# version 27
# baseline (speedup 1.0000x reference)
import numpy as np
import ml_dtypes
import concourse.bass as bass
import concourse.mybir as mybir
from concourse.bass_utils import run_bass_kernel_spmd

F32 = mybir.dt.float32
BF16 = mybir.dt.bfloat16
I32 = mybir.dt.int32
AF = mybir.ActivationFunctionType
ALU = mybir.AluOpType

S = 2048
D = 2048
HID = 5632
NKT = D // 128
NHT = HID // 128
NT = S // 128
ROT = 1024
EPS = 1e-5
KB = 1024


class Sem:
    def __init__(self, nc, name):
        self.h = nc.alloc_semaphore(name)
        self.v = 0

    def inc(self, by=1):
        self.v += by
        return self.v


def _flat(waits):
    out = []
    for w in waits:
        if w is None:
            continue
        if isinstance(w, list):
            out.extend(x for x in w if x is not None)
        else:
            out.append(w)
    return out


def build_program():
    nc = bass.Bass("TRN2", target_bir_lowering=False)

    x_in = nc.dram_tensor("x", [S, D], F32, kind="ExternalInput").ap()
    wq4 = nc.dram_tensor("wq4", [NKT, 128, NKT, 128], F32, kind="ExternalInput").ap()
    wk4 = nc.dram_tensor("wk4", [4, 128, NKT, 512], F32, kind="ExternalInput").ap()
    wv4 = nc.dram_tensor("wv4", [4, 128, NKT, 512], F32, kind="ExternalInput").ap()
    b_in = nc.dram_tensor("b_in", [3 * D], F32, kind="ExternalInput").ap()
    wo4 = nc.dram_tensor("wo4", [4, 128, NKT, 512], F32, kind="ExternalInput").ap()
    b_out = nc.dram_tensor("b_out", [D], F32, kind="ExternalInput").ap()
    gamma = nc.dram_tensor("gamma", [D], F32, kind="ExternalInput").ap()
    beta = nc.dram_tensor("beta", [D], F32, kind="ExternalInput").ap()
    wg4 = nc.dram_tensor("wg4", [NHT, 128, NKT, 128], I32, kind="ExternalInput").ap()
    wu4 = nc.dram_tensor("wu4", [NHT, 128, NKT, 128], I32, kind="ExternalInput").ap()
    wd4 = nc.dram_tensor("wd4", [4, 128, NHT, 512], I32, kind="ExternalInput").ap()
    s_g = nc.dram_tensor("s_g", [HID], F32, kind="ExternalInput").ap()
    s_u = nc.dram_tensor("s_u", [HID], F32, kind="ExternalInput").ap()
    s_d = nc.dram_tensor("s_d", [D], F32, kind="ExternalInput").ap()
    ropc = nc.dram_tensor("ropc", [S, ROT // 2], F32, kind="ExternalInput").ap()
    rops = nc.dram_tensor("rops", [S, ROT // 2], F32, kind="ExternalInput").ap()
    ident_in = nc.dram_tensor("ident", [128, 128], F32, kind="ExternalInput").ap()

    out = nc.dram_tensor("out", [S, D], F32, kind="ExternalOutput").ap()

    qts = nc.dram_tensor("qts", [NKT, 128, S], BF16, kind="Internal").ap()
    xnts = nc.dram_tensor("xnts", [NKT, 128, S], BF16, kind="Internal").ap()
    wgs4 = nc.dram_tensor("wgs4", [NHT, 128, NKT, 128], BF16, kind="Internal").ap()
    wus4 = nc.dram_tensor("wus4", [NHT, 128, NKT, 128], BF16, kind="Internal").ap()
    wds4 = nc.dram_tensor("wds4", [4, 128, NHT, 512], BF16, kind="Internal").ap()
    a_scr = nc.dram_tensor("a_scr", [1, D], F32, kind="Internal").ap()

    def sb_at(name, shape, dtype, at):
        h = nc.alloc_sbuf_tensor_at(name, shape, dtype, offset=at)
        per_part = int(np.prod(shape[1:])) * mybir.dt.size(dtype)
        return h.ap(), at + per_part

    class Region:
        def __init__(self, start):
            self.ptr = start

        def al(self, name, shape, dtype):
            at = (self.ptr + 31) // 32 * 32
            ap, end = sb_at(name, shape, dtype, at)
            self.ptr = end
            return ap

    SBUF_BASE = 16512
    LIMIT = 229344
    G = Region(SBUF_BASE)
    idtf = G.al("idtf", [128, 128], F32)
    idtb = G.al("idtb", [128, 128], BF16)
    binP = G.al("binP", [128, 48], F32)
    sgP = G.al("sgP", [128, NHT], F32)
    suP = G.al("suP", [128, NHT], F32)
    gamT = G.al("gamT", [128, NKT], F32)
    betT = G.al("betT", [128, NKT], F32)
    mnegA = G.al("mnegA", [128, NT], F32)
    rstdA = G.al("rstdA", [128, NT], F32)
    bias2A = G.al("bias2A", [128, NT], F32)
    r2A = G.al("r2A", [128, NT], F32)
    ssqKA = G.al("ssqKA", [128, 8], F32)
    ssqVA = G.al("ssqVA", [128, 8], F32)
    wvec = G.al("wvec", [128, 8], BF16)
    tmpA = G.al("tmpA", [128, 8], F32)
    tmp1A = G.al("tmp1A", [128, 4], F32)
    epsP = G.al("epsP", [128, 1], F32)
    a_T = G.al("a_T", [128, NKT], F32)
    bnst = [G.al("bnst0", [128, 24], F32), G.al("bnst1", [128, 24], F32)]
    mvst = [G.al("mvst0", [128, 2], F32), G.al("mvst1", [128, 2], F32)]
    GLOBAL_END = G.ptr

    P = Region(GLOBAL_END)
    xn2T = P.al("xn2T", [128, NKT, KB], BF16)
    pbuf = P.al("pbuf", [128, 8, D], BF16)
    binBK = P.al("binBK", [128, D], F32)
    binBV = P.al("binBV", [128, D], F32)
    _xoff = (P.ptr + 31) // 32 * 32
    x_t = [P.al("x_t0", [128, D], F32), P.al("x_t1", [128, D], F32)]
    _xnoff = (P.ptr + 31) // 32 * 32
    xn_t = [P.al("xn_t0", [128, D], F32), P.al("xn_t1", [128, D], F32)]
    xnr2b = [P.al(f"xnr2b{i}", [128, D], BF16) for i in range(4)]
    kvbK2, _ = sb_at("kvbK2", [128, NKT, 512], BF16, _xoff)
    kvbV2, _ = sb_at("kvbV2", [128, NKT, 512], BF16, _xnoff)
    xntg = P.al("xntg", [128, NKT, 256], BF16)
    a_row = P.al("a_row", [1, D], F32)
    kvbK = P.al("kvbK", [128, NKT, 512], BF16)
    kvbV = P.al("kvbV", [128, NKT, 512], BF16)
    wqB = [P.al("wqB0", [128, NKT, 128], BF16), P.al("wqB1", [128, NKT, 128], BF16)]
    qtstg = [P.al("qtstg0", [128, 512], BF16), P.al("qtstg1", [128, 512], BF16)]
    krst = [P.al("krst0", [128, 512], F32), P.al("krst1", [128, 512], F32)]
    vrst = [P.al("vrst0", [128, 512], F32), P.al("vrst1", [128, 512], F32)]
    sqscr = P.al("sqscr", [128, 512], F32)
    vscr = P.al("vscr", [128, 512], F32)
    ropcB = [P.al("ropcB0", [128, 256], F32), P.al("ropcB1", [128, 256], F32)]
    ropsB = [P.al("ropsB0", [128, 256], F32), P.al("ropsB1", [128, 256], F32)]
    assert P.ptr <= LIMIT, f"P1 over SBUF: {P.ptr}"

    F = Region(GLOBAL_END)
    hT = F.al("hT", [128, NHT, KB], BF16)
    HT_END = F.ptr
    xnT = F.al("xnT", [128, NKT, KB], BF16)
    wgB = [F.al("wgB0", [128, NKT, 128], BF16), F.al("wgB1", [128, NKT, 128], BF16)]
    wuB = [F.al("wuB0", [128, NKT, 128], BF16), F.al("wuB1", [128, NKT, 128], BF16)]
    sigstg = F.al("sigstg", [128, 512], F32)
    upstg = F.al("upstg", [128, 512], F32)
    assert F.ptr <= LIMIT, f"F1 over SBUF: {F.ptr}"

    F2 = Region(HT_END)
    wdB = F2.al("wdB", [128, NHT, 512], BF16)
    woB = F2.al("woB", [128, NKT, 512], BF16)
    qtg = F2.al("qtg", [128, NKT, 512], BF16)
    b_outB = F2.al("b_outB", [128, D], F32)
    s_dB = F2.al("s_dB", [128, D], F32)
    x_ij = [F2.al("x_ij0", [128, 512], F32), F2.al("x_ij1", [128, 512], F32)]
    ffnstg = F2.al("ffnstg", [128, 512], F32)
    gstg = F2.al("gstg", [128, 512], F32)
    xnstg = F2.al("xnstg", [128, 512], F32)
    s1stg = F2.al("s1stg", [128, 512], F32)
    ostg = [F2.al("ostg0", [128, 512], F32), F2.al("ostg1", [128, 512], F32)]
    assert F2.ptr <= LIMIT, f"F2C over SBUF: {F2.ptr}"

    psX = [nc.alloc_psum_tensor(f"psX{i}", [128, 512], F32).ap() for i in range(4)]
    psT = [nc.alloc_psum_tensor(f"psT{i}", [128, 512], F32).ap() for i in range(2)]
    psA = nc.alloc_psum_tensor("psA", [128, 512], F32).ap()

    dmaI = Sem(nc, "dmaI")
    xldS = [Sem(nc, "xld0"), Sem(nc, "xld1")]
    ropeS = [Sem(nc, "rope0"), Sem(nc, "rope1")]
    sxnS = Sem(nc, "sxn")
    wqS = [Sem(nc, "wq0"), Sem(nc, "wq1")]
    sqtS = [Sem(nc, "sqt0"), Sem(nc, "sqt1")]
    kvS = [Sem(nc, "kv0"), Sem(nc, "kv1")]
    saS = Sem(nc, "saS")
    aTS = Sem(nc, "aTS")
    bsS = Sem(nc, "bsS")
    xnlS = Sem(nc, "xnlS")
    wguS = [Sem(nc, "wgu0"), Sem(nc, "wgu1")]
    wglS = [Sem(nc, "wgl0"), Sem(nc, "wgl1")]
    swgS = [Sem(nc, "swg0"), Sem(nc, "swg1")]
    wdlS = [Sem(nc, f"wdl{s}") for s in range(4)]
    wdl2S = [Sem(nc, f"wdm{s}") for s in range(4)]
    swdS = [Sem(nc, f"swd{s}") for s in range(4)]
    woS = Sem(nc, "woS")
    qtlS = Sem(nc, "qtlS")
    xijS = [Sem(nc, "xij0"), Sem(nc, "xij1")]
    ibS = Sem(nc, "ibS")
    outS = [Sem(nc, "out0"), Sem(nc, "out1")]
    semT = Sem(nc, "semT")
    semV = Sem(nc, "semV")
    semS = Sem(nc, "semS")
    semG = Sem(nc, "semG")
    semW = Sem(nc, "semW")
    wctr = [0]

    def vg(e, ins):
        wctr[0] += 1
        ins.then_inc(semW.h, 1)
        e.wait_ge(semW.h, wctr[0])

    semY = Sem(nc, "semY")
    yctr = [0]

    def sg(e, ins):
        yctr[0] += 1
        ins.then_inc(semY.h, 1)
        e.wait_ge(semY.h, yctr[0])

    plan = {"sync": [], "gpsimd": [], "vector": [], "scalar": [], "tensor": []}

    def dma(q, out_ap, in_ap, sem, waits=(), slow=False):
        wl = _flat(waits)

        def f(e, wl=wl, out_ap=out_ap, in_ap=in_ap, sem=sem, slow=slow):
            for (s, v) in wl:
                if v > 0:
                    e.wait_ge(s.h, v)
            e.dma_start(out_ap, in_ap,
                        allow_slow_non_contiguous=slow).then_inc(sem.h, 16)

        plan[q].append(f)
        sem.inc(16)
        return (sem, sem.v)

    eng_chain = {"vector": semV, "scalar": semS, "gpsimd": semG}

    def oneng(eng, fn, sem=None, waits=()):
        ch = eng_chain.get(eng)
        if ch is not None:
            assert sem is None or sem is ch
            sem = ch
        wl = _flat(list(waits) + ([(ch, ch.v)] if ch is not None else []))
        byid = {}
        for (s, v) in wl:
            if v > byid.get(id(s), (None, 0))[1]:
                byid[id(s)] = (s, v)
        wl = list(byid.values())
        if sem is not None:
            sem.inc(1)

        def f(e, wl=wl, fn=fn, sem=sem):
            for (s, v) in wl:
                if v > 0:
                    e.wait_ge(s.h, v)
            ins = fn(e)
            if sem is not None:
                assert ins is not None, "producer fn must return its last instr"
                ins.then_inc(sem.h, 1)

        plan[eng].append(f)
        return (sem, sem.v) if sem is not None else None

    bank_free = {}
    bank_ready = {}

    def mm_group(ps, mms, waits=(), start=True, stop=True):
        wl = _flat(list(waits) + ([bank_free.get(id(ps))] if start else []))

        def f(e, wl=wl, mms=mms, start=start, stop=stop):
            for (s, v) in wl:
                if v > 0:
                    e.wait_ge(s.h, v)
            n = len(mms)
            ins = None
            for i, (l, r, o) in enumerate(mms):
                ins = e.matmul(o if o is not None else ps, l, r,
                               start=(start and i == 0),
                               stop=(stop and i == n - 1))
            ins.then_inc(semT.h, 1)

        plan["tensor"].append(f)
        semT.inc(1)
        evt = (semT, semT.v)
        bank_ready[id(ps)] = evt
        return evt

    def tp_group(ps, items, waits=()):
        wl = _flat(list(waits) + [bank_free.get(id(ps))])

        def f(e, wl=wl, items=items):
            for (s, v) in wl:
                if v > 0:
                    e.wait_ge(s.h, v)
            ins = None
            for (src, o) in items:
                ins = e.transpose(o, src, idtf)
            ins.then_inc(semT.h, 1)

        plan["tensor"].append(f)
        semT.inc(1)
        evt = (semT, semT.v)
        bank_ready[id(ps)] = evt
        return evt

    def evac(eng_name, ps, fn, sem, waits=()):
        evt = oneng(eng_name, fn, sem=sem,
                    waits=list(waits) + [bank_ready[id(ps)]])
        bank_free[id(ps)] = evt
        return evt

    dma("sync", idtf, ident_in, dmaI)
    idtb_done = dma("gpsimd", idtb, ident_in, ibS)
    dma("sync", binP, b_in.rearrange("(m p) -> p m", p=128), dmaI, slow=True)
    dma("sync", sgP, s_g.rearrange("(m p) -> p m", p=128), dmaI, slow=True)
    dma("sync", suP, s_u.rearrange("(m p) -> p m", p=128), dmaI, slow=True)
    dma("sync", gamT, gamma.rearrange("(k p) -> p k", p=128), dmaI, slow=True)
    dma("sync", betT, beta.rearrange("(k p) -> p k", p=128), dmaI, slow=True)
    dma("sync", binBK, b_in[D:2 * D].rearrange("(o d) -> o d", o=1).broadcast_to([128, D]), dmaI)
    init_done = dma("sync", binBV,
                    b_in[2 * D:3 * D].rearrange("(o d) -> o d", o=1).broadcast_to([128, D]),
                    dmaI)
    a_zero_evt = oneng("vector", lambda e: e.memset(a_row, 0.0), sem=semV,
                       waits=[init_done])
    eps_evt = oneng("vector", lambda e: e.memset(epsP, EPS), sem=semV)
    def load_set6(e):
        return e.add_instruction(mybir.InstLoadActFuncSet(
            name=nc.get_next_instruction_name(), act_func_set_id=6,
            ins=[], outs=[]))
    oneng("scalar", load_set6, waits=[init_done])

    x_cons = [None, None]
    xn_cons = [None, None]
    xr2b_evt = [None] * 4
    xr2b_cons = [None]
    bnst_free = [None, None]
    xntg_free = [None]
    pend_spill = [None]
    xn2T_guard = [None]
    kv2_guard = [None]

    def p1_stepA_tile(h, il):
        gi = h * 8 + il
        par = gi % 2
        sl = il % 2
        quad = il % 4
        tb = par * 4
        lnc = tmpA[:, tb + 0:tb + 1]
        t2 = tmpA[:, tb + 1:tb + 2]
        vcol = tmpA[:, tb + 2:tb + 3]

        alias_g = [kv2_guard[0]] if (h == 1 and il < 2) else []
        x_evt = dma("sync", x_t[par], x_in[gi * 128:(gi + 1) * 128, :],
                    xldS[par], waits=[x_cons[par]] + alias_g)
        if pend_spill[0] is not None:
            g2d, evd = pend_spill[0]
            pend_spill[0] = None
            last = None
            for part in range(4):
                last = dma("sync",
                           xnts[part * 4:(part + 1) * 4, :,
                                g2d * 256:(g2d + 1) * 256]
                           .rearrange("k p t -> p k t"),
                           xntg[:, part * 4:(part + 1) * 4, :],
                           sxnS, waits=evd if part == 0 else [])
            xntg_free[0] = last

        def bnf(e, par=par):
            ins = None
            for c in range(4):
                ins = e.bn_stats(bnst[par][:, c * 6:(c + 1) * 6],
                                 x_t[par][:, c * 512:(c + 1) * 512])
                if c < 3:
                    vg(e, ins)
            return ins
        bn_done = oneng("vector", bnf, sem=semV,
                        waits=[x_evt, bnst_free[par]])
        agg_done = oneng("vector",
                         lambda e, par=par: e.bn_aggr(mvst[par], bnst[par]),
                         sem=semV, waits=[bn_done])
        bnst_free[par] = agg_done

        ev1 = oneng("vector",
                    lambda e, par=par, gi=gi: e.tensor_scalar(
                        mnegA[:, gi:gi + 1], mvst[par][:, 0:1],
                        -1.0, None, ALU.mult),
                    sem=semV, waits=[agg_done])

        def r1f(e, par=par, gi=gi, lnc=lnc, vcol=vcol):
            sg(e, e.activation(vcol, mvst[par][:, 1:2], AF.Copy))
            sg(e, e.activation(lnc, mvst[par][:, 1:2], AF.Ln, bias=epsP))
            return e.activation(rstdA[:, gi:gi + 1], lnc, AF.Exp, scale=-0.5)
        sc1 = oneng("scalar", r1f, sem=semS, waits=[agg_done, eps_evt])

        def fn2(e, par=par, gi=gi, t2=t2, vcol=vcol):
            vg(e, e.tensor_tensor(bias2A[:, gi:gi + 1], mnegA[:, gi:gi + 1],
                                  rstdA[:, gi:gi + 1], ALU.mult))
            vg(e, e.tensor_tensor(t2, vcol,
                                  rstdA[:, gi:gi + 1], ALU.mult))
            return e.tensor_tensor(t2, t2, rstdA[:, gi:gi + 1], ALU.mult)
        ev2 = oneng("vector", fn2, sem=semV, waits=[sc1, ev1])

        def r2f(e, gi=gi, lnc=lnc, t2=t2):
            sg(e, e.activation(lnc, t2, AF.Ln, bias=epsP))
            return e.activation(r2A[:, gi:gi + 1], lnc, AF.Exp, scale=-0.5)
        sc2 = oneng("scalar", r2f, sem=semS, waits=[ev2])

        xn_evt = oneng("gpsimd",
                       lambda e, par=par, gi=gi: e.tensor_scalar(
                           xn_t[par], x_t[par], mnegA[:, gi:gi + 1],
                           rstdA[:, gi:gi + 1], ALU.add, ALU.mult),
                       sem=semG, waits=[x_evt, ev1, sc1, xn_cons[par]] + alias_g)
        x_cons[par] = [bn_done, xn_evt]

        xr2_evt = oneng("scalar",
                        lambda e, par=par, gi=gi, quad=quad: e.activation(
                            xnr2b[quad], xn_t[par], AF.Identity,
                            scale=r2A[:, gi:gi + 1]),
                        sem=semS, waits=[xn_evt, sc2, xr2b_cons[0]])
        xr2b_evt[quad] = xr2_evt

        last_tp = None
        for g in range(4):
            ps = psT[g % 2]
            items = [(xn_t[par][:, (4 * g + q) * 128:(4 * g + q + 1) * 128],
                      ps[:, q * 128:(q + 1) * 128]) for q in range(4)]
            last_tp = tp_group(ps, items, waits=[xn_evt, init_done])
            ew = [xntg_free[0]] if (sl == 0 and g == 0) else []

            def ev(e, ps=ps, g=g, sl=sl):
                return e.tensor_copy(
                    xntg[:, 4 * g:4 * (g + 1), sl * 128:(sl + 1) * 128],
                    ps.rearrange("p (q t) -> p q t", q=4))
            evac("vector", ps, ev, semV, waits=ew)
        xn_cons[par] = [xr2_evt, last_tp]

        if sl == 1:
            pend_spill[0] = (gi // 2, [(semV, semV.v), (semS, semS.v)])

        if quad == 3:
            a4 = il // 4
            last_tp2 = None
            for kt in range(NKT):
                ps = psX[kt % 4]
                mms = [(xnr2b[q][:, kt * 128:(kt + 1) * 128], idtb,
                        ps[:, q * 128:(q + 1) * 128]) for q in range(4)]
                w0 = list(xr2b_evt) + [idtb_done]
                last_tp2 = mm_group(ps, mms, waits=w0)
                gw = [xn2T_guard[0]] if (h == 1 and a4 == 0 and kt < 4) else []

                def fxs(e, ps=ps, kt=kt, a4=a4):
                    return e.activation(
                        xn2T[:, kt, a4 * 512:(a4 + 1) * 512], ps, AF.Identity,
                        scale=gamT[:, kt:kt + 1], bias=betT[:, kt:kt + 1])
                evac("scalar", ps, fxs, semS, waits=gw)
            xr2b_cons[0] = last_tp2

    wq_cons = [None, None]
    stg_free = [None, None]

    def p1_stepB(h, a_done):
        for m in range(NKT):
            sl = m % 2
            w_evt = dma("gpsimd", wqB[sl], wq4[m],
                        wqS[sl],
                        waits=[wq_cons[sl]] + ([init_done] if m == 0 else []))
            last_grp = None
            for n in range(2):
                ps = psX[(2 * m + n) % 4]
                mms = [(wqB[sl][:, k, :], xn2T[:, k, n * 512:(n + 1) * 512], None)
                       for k in range(NKT)]
                extra = a_done if (m == 0 and n == 0) else []
                last_grp = mm_group(ps, mms, waits=[w_evt] + extra)
                st = qtstg[n]

                def ev(e, ps=ps, st=st, m=m):
                    return e.activation(st, ps, AF.Identity,
                                        bias=binP[:, m:m + 1])
                vv = evac("scalar", ps, ev, semS, waits=[stg_free[n]])
                stg_free[n] = dma("sync",
                                  qts[m, :, h * KB + n * 512:h * KB + (n + 1) * 512],
                                  st, sqtS[n], waits=[vv])
            wq_cons[sl] = last_grp

    kv_cons = [None, None]
    rope_cons = [None, None]

    def p1_stepC(h, a_eng_done):
        k_free = [None, None]
        v_free = [None, None]
        pbuf_guard = (semT, semT.v)
        z1 = oneng("vector", lambda e: e.memset(ssqKA, 0.0), sem=semV,
                   waits=[pbuf_guard, init_done])
        z2 = oneng("vector", lambda e: e.memset(ssqVA, 0.0), sem=semV)
        zevt = z2
        for c in range(4):
            slot = c % 2
            bufK = kvbK if slot == 0 else kvbK2
            bufV = kvbV if slot == 0 else kvbV2
            w0 = [kv_cons[slot]]
            if c == 1:
                w0 = w0 + a_eng_done
            dma("gpsimd", bufK, wk4[c], kvS[slot], waits=w0)
            kv_evt = dma("gpsimd", bufV, wv4[c], kvS[slot], waits=[])
            for il in range(8):
                gi = h * 8 + il
                rsl = il % 2
                kb = krst[rsl]
                vb = vrst[rsl]
                rope_evt = None
                if c < 2:
                    dma("sync", ropcB[rsl],
                        ropc[gi * 128:(gi + 1) * 128, c * 256:(c + 1) * 256],
                        ropeS[rsl], waits=[rope_cons[rsl]])
                    rope_evt = dma("sync", ropsB[rsl],
                                   rops[gi * 128:(gi + 1) * 128,
                                        c * 256:(c + 1) * 256],
                                   ropeS[rsl], waits=[])
                psK = psX[(2 * il) % 4]
                psV = psX[(2 * il + 1) % 4]
                mm_group(psK, [(xn2T[:, k, il * 128:(il + 1) * 128],
                                bufK[:, k, :], None) for k in range(NKT)],
                         waits=[kv_evt])
                mm_group(psV, [(xn2T[:, k, il * 128:(il + 1) * 128],
                                bufV[:, k, :], None) for k in range(NKT)],
                         waits=[kv_evt])
                if il == 7:
                    kv_cons[slot] = bank_ready[id(psV)]

                def kpath(e, psK=psK, c=c, rsl=rsl, kb=kb):
                    ins = e.tensor_tensor(kb, psK,
                                          binBK[:, c * 512:(c + 1) * 512], ALU.add)
                    if c < 2:
                        vg(e, ins)
                        cs, sn = ropcB[rsl], ropsB[rsl]
                        ke = kb[:, 0:512:2]
                        ko = kb[:, 1:512:2]
                        vg(e, e.tensor_tensor(vscr[:, 0:256], ke, sn, ALU.mult))
                        vg(e, e.tensor_tensor(vscr[:, 256:512], ko, sn, ALU.mult))
                        vg(e, e.tensor_tensor(ke, ke, cs, ALU.mult))
                        vg(e, e.tensor_tensor(ko, ko, cs, ALU.mult))
                        vg(e, e.tensor_tensor(ke, ke, vscr[:, 256:512], ALU.subtract))
                        ins = e.tensor_tensor(ko, ko, vscr[:, 0:256], ALU.add)
                    return ins
                kw = [bank_ready[id(psK)], zevt, k_free[rsl]]
                if c < 2:
                    kw.append(rope_evt)
                vvK = oneng("vector", kpath, sem=semV, waits=kw)
                bank_free[id(psK)] = vvK
                sS = oneng("scalar",
                           lambda e, kb=kb: e.activation(
                               sqscr, kb, AF.Square, accum_out=tmp1A[:, 0:1]),
                           sem=semS, waits=[vvK])
                k_free[rsl] = sS
                vv2 = oneng("vector",
                            lambda e, il=il: e.tensor_tensor(
                                ssqKA[:, il:il + 1], ssqKA[:, il:il + 1],
                                tmp1A[:, 0:1], ALU.add),
                            sem=semV, waits=[sS])

                def vpath(e, psV=psV, c=c, rsl=rsl, vb=vb):
                    ins = e.tensor_tensor(vb, psV,
                                          binBV[:, c * 512:(c + 1) * 512], ALU.add)
                    if c < 2:
                        vg(e, ins)
                        cs, sn = ropcB[rsl], ropsB[rsl]
                        ve_ = vb[:, 0:512:2]
                        vo = vb[:, 1:512:2]
                        vg(e, e.tensor_tensor(vscr[:, 0:256], ve_, sn, ALU.mult))
                        vg(e, e.tensor_tensor(vscr[:, 256:512], vo, sn, ALU.mult))
                        vg(e, e.tensor_tensor(ve_, ve_, cs, ALU.mult))
                        vg(e, e.tensor_tensor(vo, vo, cs, ALU.mult))
                        vg(e, e.tensor_tensor(ve_, ve_, vscr[:, 256:512], ALU.subtract))
                        ins = e.tensor_tensor(vo, vo, vscr[:, 0:256], ALU.add)
                    return ins
                vw = [bank_ready[id(psV)], v_free[rsl]]
                if c < 2:
                    vw.append(rope_evt)
                vvV = oneng("vector", vpath, sem=semV, waits=vw)
                bank_free[id(psV)] = vvV
                if c < 2:
                    rope_cons[rsl] = vvV
                sS2 = oneng("scalar",
                            lambda e, vb=vb: e.activation(
                                sqscr, vb, AF.Square, accum_out=tmp1A[:, 1:2]),
                            sem=semS, waits=[vvV])
                vv4 = oneng("vector",
                            lambda e, il=il: e.tensor_tensor(
                                ssqVA[:, il:il + 1], ssqVA[:, il:il + 1],
                                tmp1A[:, 1:2], ALU.add),
                            sem=semV, waits=[sS2])
                pv = oneng("vector",
                           lambda e, il=il, c=c, kb=kb, vb=vb: e.tensor_tensor(
                               pbuf[:, il, c * 512:(c + 1) * 512], kb, vb,
                               ALU.mult),
                           sem=semV, waits=[vv2, vv4])
                v_free[rsl] = [sS2, pv]

        wlast = [None, None]
        for il in range(8):
            p2 = il % 2
            vv = oneng("vector",
                       lambda e, il=il, p2=p2: e.tensor_tensor(
                           tmp1A[:, p2:p2 + 1], ssqKA[:, il:il + 1],
                           ssqVA[:, il:il + 1], ALU.mult),
                       sem=semV, waits=[wlast[p2]])

            def wf(e, il=il, p2=p2):
                sg(e, e.activation(tmp1A[:, 2 + p2:3 + p2],
                                   tmp1A[:, p2:p2 + 1], AF.Ln))
                return e.activation(wvec[:, il:il + 1],
                                    tmp1A[:, 2 + p2:3 + p2], AF.Exp,
                                    scale=-0.5)
            wlast[p2] = oneng("scalar", wf, sem=semS, waits=[vv])
        w_ready = [(semS, semS.v)]
        for c2 in range(4):
            mms = [(wvec[:, il:il + 1], pbuf[:, il, c2 * 512:(c2 + 1) * 512],
                    psA[0:1, :]) for il in range(8)]
            mm_group(psA, mms, waits=w_ready + [a_zero_evt])

            def aev(e, c2=c2):
                return e.tensor_tensor(a_row[0:1, c2 * 512:(c2 + 1) * 512],
                                       a_row[0:1, c2 * 512:(c2 + 1) * 512],
                                       psA[0:1, :], ALU.add)
            evac("vector", psA, aev, semV)

    for h in range(2):
        for il in range(8):
            p1_stepA_tile(h, il)
        if pend_spill[0] is not None:
            g2d, evd = pend_spill[0]
            pend_spill[0] = None
            last = None
            for part in range(4):
                last = dma("sync",
                           xnts[part * 4:(part + 1) * 4, :,
                                g2d * 256:(g2d + 1) * 256]
                           .rearrange("k p t -> p k t"),
                           xntg[:, part * 4:(part + 1) * 4, :],
                           sxnS, waits=evd if part == 0 else [])
            xntg_free[0] = last
        a_done = [(semS, semS.v), (semV, semV.v)]
        a_eng_done = [(semS, semS.v), (semV, semV.v), (semT, semT.v),
                      (semG, semG.v)]
        p1_stepB(h, a_done)
        p1_stepC(h, a_eng_done)
        kv2_guard[0] = kv_cons[1]
        xn2T_guard[0] = (semT, semT.v)

    vv = oneng("vector",
               lambda e: e.tensor_scalar(a_row, a_row, 0.5, 0.5,
                                         ALU.mult, ALU.add),
               sem=semV)
    ase = dma("sync", a_scr, a_row, saS, waits=[vv])
    aT_ready = dma("sync", a_T, a_scr.rearrange("o (k p) -> p (o k)", p=128),
                   aTS, waits=[ase], slow=True)

    p1_all_done = [(semV, semV.v), (semS, semS.v), (semT, semT.v),
                   (semG, semG.v), (sxnS, sxnS.v),
                   (sqtS[0], sqtS[0].v), (sqtS[1], sqtS[1].v)]
    dma("sync", b_outB,
        b_out.rearrange("(o d) -> o d", o=1).broadcast_to([128, D]), bsS,
        waits=p1_all_done)
    bs_loaded = dma("sync", s_dB,
                    s_d.rearrange("(o d) -> o d", o=1).broadcast_to([128, D]),
                    bsS)

    wf_cons = [None, None]
    wf_spill = [None, None]
    hs_free = [None]

    def f1(h):
        if h == 0:
            oneng("scalar", lambda e: e.preload_act_table(AF.Silu))
        f2c_guard = ([(semT, semT.v)] +
                     [(s, s.v) for s in swdS]) if h == 1 else []
        xnT_evt = None
        for part in range(4):
            w0 = []
            if part == 0:
                w0 = p1_all_done if h == 0 else f2c_guard
            xnT_evt = dma("sync", xnT[:, part * 4:(part + 1) * 4, :],
                          xnts[part * 4:(part + 1) * 4, :, h * KB:(h + 1) * KB]
                          .rearrange("k p t -> p k t"),
                          xnlS, waits=w0)
        for m in range(NHT):
            sl = m % 2
            reuse = [wf_cons[sl], wf_spill[sl]]
            if h == 0:
                dma("gpsimd", wgB[sl], wg4[m], wguS[sl],
                    waits=[xnT_evt] + reuse)
                w_evt = dma("gpsimd", wuB[sl], wu4[m], wguS[sl], waits=[])

                def sub128(e, sl=sl):
                    vg(e, e.tensor_scalar(wgB[sl].rearrange("p k c -> p (k c)"),
                                          wgB[sl].rearrange("p k c -> p (k c)"),
                                          -128.0, None, ALU.add))
                    return e.tensor_scalar(wuB[sl].rearrange("p k c -> p (k c)"),
                                           wuB[sl].rearrange("p k c -> p (k c)"),
                                           -128.0, None, ALU.add)
                w_evt = oneng("vector", sub128, sem=semV, waits=[w_evt])
                dma("sync", wgs4[m], wgB[sl], swgS[sl], waits=[w_evt])
                wf_spill[sl] = dma("sync", wus4[m], wuB[sl], swgS[sl],
                                   waits=[])
            else:
                dma("sync", wgB[sl], wgs4[m], wglS[sl],
                    waits=([xnT_evt] + reuse))
                w_evt = dma("sync", wuB[sl], wus4[m], wglS[sl], waits=[])
                wf_spill[sl] = None
            lastg = None
            for n in range(2):
                psG = psX[(2 * m + n) % 2]
                psU = psX[2 + (2 * m + n) % 2]
                mm_group(psG, [(wgB[sl][:, k, :],
                                xnT[:, k, n * 512:(n + 1) * 512], None)
                               for k in range(NKT)], waits=[w_evt])
                lastg = mm_group(psU, [(wuB[sl][:, k, :],
                                        xnT[:, k, n * 512:(n + 1) * 512], None)
                                       for k in range(NKT)], waits=[w_evt])
                sSe = evac("scalar", psG,
                           lambda e, psG=psG, m=m: e.activation(
                               sigstg, psG, AF.Silu, scale=sgP[:, m:m + 1]),
                           semS, waits=[hs_free[0]])
                vVe = evac("vector", psU,
                           lambda e, psU=psU, m=m: e.tensor_scalar(
                               upstg, psU, suP[:, m:m + 1], None, ALU.mult),
                           semV, waits=[hs_free[0]])
                vv2 = oneng("vector",
                            lambda e, m=m, n=n: e.tensor_tensor(
                                hT[:, m, n * 512:(n + 1) * 512], sigstg, upstg,
                                ALU.mult),
                            sem=semV, waits=[sSe, vVe])
                hs_free[0] = vv2
                bank_free[id(psG)] = vv2
                bank_free[id(psU)] = vv2
            wf_cons[sl] = lastg

    wd_seg_cons = [None] * 4
    wd_seg_spill = [None] * 4
    wo_cons = [None]
    qt_cons = [None]
    xij_free = [None, None]
    gs_free = [None]
    cmb_free = [None]
    SEGS = [(11 * s, [(11 * s, 4), (11 * s + 4, 4), (11 * s + 8, 3)])
            for s in range(4)]

    def f2c(h):
        f1_done = [(semV, semV.v), (semT, semT.v)]
        if h == 0:
            f1_done += [(swgS[0], swgS[0].v), (swgS[1], swgS[1].v)]
        for j in range(4):
            wd_seg_evt = [None] * 4
            for s, (base, parts) in enumerate(SEGS):
                w0 = [wd_seg_cons[s], wd_seg_spill[s]]
                if j == 0:
                    w0 = w0 + f1_done
                if h == 0:
                    ld = dma("gpsimd", wdB[:, base:base + 11, :],
                             wd4[j, :, base:base + 11, :],
                             wdlS[s], waits=w0)
                    sub = oneng("vector",
                                lambda e, base=base: e.tensor_scalar(
                                    wdB[:, base:base + 11, :]
                                    .rearrange("p k c -> p (k c)"),
                                    wdB[:, base:base + 11, :]
                                    .rearrange("p k c -> p (k c)"),
                                    -128.0, None, ALU.add),
                                sem=semV, waits=[ld])
                    wd_seg_spill[s] = dma("sync",
                                          wds4[j, :, base:base + 11, :],
                                          wdB[:, base:base + 11, :],
                                          swdS[s], waits=[sub])
                    wd_seg_evt[s] = sub
                else:
                    ld = dma("sync", wdB[:, base:base + 11, :],
                             wds4[j, :, base:base + 11, :],
                             wdl2S[s], waits=w0)
                    wd_seg_spill[s] = None
                    wd_seg_evt[s] = ld
            wo_evt = dma("gpsimd", woB, wo4[j], woS,
                         waits=f1_done + [wo_cons[0]])

            last_bank = None
            for g in range(2):
                qt_evt = None
                for part in range(4):
                    qt_evt = dma("sync", qtg[:, part * 4:(part + 1) * 4, :],
                                 qts[part * 4:(part + 1) * 4, :,
                                     h * KB + g * 512:h * KB + (g + 1) * 512]
                                 .rearrange("k p t -> p k t"),
                                 qtlS,
                                 waits=([aT_ready, qt_cons[0]] + f1_done)
                                 if part == 0 else [])

                def ascale(e):
                    ins = None
                    for k in range(NKT):
                        ins = e.tensor_scalar(qtg[:, k, :], qtg[:, k, :],
                                              a_T[:, k:k + 1], None, ALU.mult)
                        if k < NKT - 1:
                            vg(e, ins)
                    return ins
                aq_evt = oneng("vector", ascale, sem=semV, waits=[qt_evt])
                for i4 in range(4):
                    il = g * 4 + i4
                    gi = h * 8 + il
                    sl = il % 2
                    psD = psX[sl]
                    psY = psX[2 + sl]
                    for s, (base, _) in enumerate(SEGS):
                        ev = mm_group(psD,
                                      [(hT[:, k, il * 128:(il + 1) * 128],
                                        wdB[:, k, :], None)
                                       for k in range(base, base + 11)],
                                      waits=[wd_seg_evt[s]],
                                      start=(s == 0), stop=(s == 3))
                        wd_seg_cons[s] = ev
                    last_bank = mm_group(psY,
                                         [(qtg[:, k, i4 * 128:(i4 + 1) * 128],
                                           woB[:, k, :], None)
                                          for k in range(NKT)],
                                         waits=[aq_evt, wo_evt])
                    x_evt = dma("sync", x_ij[sl],
                                x_in[gi * 128:(gi + 1) * 128,
                                     j * 512:(j + 1) * 512],
                                xijS[sl], waits=[xij_free[sl]])
                    bsw = [bs_loaded] if (h == 0 and j == 0 and il == 0) else []
                    vv = evac("vector", psD,
                              lambda e, psD=psD, j=j: e.tensor_tensor(
                                  ffnstg, psD,
                                  s_dB[:, j * 512:(j + 1) * 512], ALU.mult),
                              semV, waits=bsw + [gs_free[0]])
                    sS = oneng("scalar",
                               lambda e: e.activation(gstg, ffnstg, AF.Silu),
                               sem=semS, waits=[vv, cmb_free[0]])
                    gs_free[0] = sS
                    sS2 = oneng("scalar",
                                lambda e, sl=sl, gi=gi: e.activation(
                                    xnstg, x_ij[sl], AF.Identity,
                                    scale=rstdA[:, gi:gi + 1],
                                    bias=bias2A[:, gi:gi + 1]),
                                sem=semS, waits=[x_evt, cmb_free[0]])

                    def comb(e, psY=psY, sl=sl, j=j):
                        vg(e, e.tensor_tensor(s1stg, xnstg,
                                              b_outB[:, j * 512:(j + 1) * 512],
                                              ALU.add))
                        vg(e, e.tensor_tensor(s1stg, s1stg, psY, ALU.add))
                        vg(e, e.tensor_tensor(s1stg, s1stg, gstg, ALU.mult))
                        return e.tensor_tensor(ostg[sl], s1stg, x_ij[sl],
                                               ALU.add)
                    vv2 = evac("vector", psY, comb, semV, waits=[sS, sS2])
                    xij_free[sl] = vv2
                    cmb_free[0] = vv2
                    odone = dma("sync",
                                out[gi * 128:(gi + 1) * 128,
                                    j * 512:(j + 1) * 512],
                                ostg[sl], outS[sl], waits=[vv2])
                    bank_free[id(psY)] = odone
                qt_cons[0] = last_bank
            wo_cons[0] = last_bank

    for h in range(2):
        f1(h)
        f2c(h)

    out_tot = [(s, s.v) for s in outS]
    spill_tot = [(s, s.v) for s in [sxnS, sqtS[0], sqtS[1], swgS[0], swgS[1],
                                    saS] + swdS]

    def fin(e):
        for (s, v) in out_tot + spill_tot:
            if v > 0:
                e.wait_ge(s.h, v)
    plan["sync"].append(fin)

    with nc.Block() as block:
        @block.sync
        def _(e):
            for f in plan["sync"]:
                f(e)

        @block.gpsimd
        def _(e):
            for f in plan["gpsimd"]:
                f(e)

        @block.vector
        def _(e):
            for f in plan["vector"]:
                f(e)

        @block.scalar
        def _(e):
            for f in plan["scalar"]:
                f(e)

        @block.tensor
        def _(e):
            for f in plan["tensor"]:
                f(e)

    return nc


def _host_consts():
    i = np.arange(0, ROT, 2, dtype=np.float64) / ROT
    inv = 1.0 / np.power(10000.0, i)
    t = np.arange(S, dtype=np.float64)[:, None]
    fr = t * inv[None, :]
    ropc_np = np.cos(fr).astype(np.float32)
    rops_np = np.sin(fr).astype(np.float32)
    ident = np.eye(128, dtype=np.float32)
    return ropc_np, rops_np, ident


def kernel(L0_attn_X, L0_attn_W_in, L0_attn_b_in, L0_attn_W_out, L0_attn_b_out,
           L0_attn_gamma, L0_attn_beta, L0_ffn_wq_gate, L0_ffn_s_gate,
           L0_ffn_wq_up, L0_ffn_s_up, L0_ffn_wq_down, L0_ffn_s_down):
    ropc_np, rops_np, ident = _host_consts()
    W_in = np.asarray(L0_attn_W_in, dtype=np.float32)
    W_out = np.asarray(L0_attn_W_out, dtype=np.float32)
    Wg = np.asarray(L0_ffn_wq_gate, dtype=np.int32)
    Wu = np.asarray(L0_ffn_wq_up, dtype=np.int32)
    Wd = np.asarray(L0_ffn_wq_down, dtype=np.int32)
    C = lambda a: np.ascontiguousarray(a)
    shared = dict(
        wq4=C(W_in[:, :D].reshape(16, 128, 16, 128).transpose(2, 1, 0, 3)),
        wk4=C(W_in[:, D:2 * D].reshape(16, 128, 4, 512).transpose(2, 1, 0, 3)),
        wv4=C(W_in[:, 2 * D:].reshape(16, 128, 4, 512).transpose(2, 1, 0, 3)),
        wo4=C(W_out.reshape(16, 128, 4, 512).transpose(2, 1, 0, 3)),
        wg4=C(Wg.reshape(44, 128, 16, 128).transpose(0, 3, 2, 1)),
        wu4=C(Wu.reshape(44, 128, 16, 128).transpose(0, 3, 2, 1)),
        wd4=C(Wd.reshape(4, 512, 44, 128).transpose(0, 3, 2, 1)),
        b_in=C(np.asarray(L0_attn_b_in, dtype=np.float32)),
        b_out=C(np.asarray(L0_attn_b_out, dtype=np.float32)),
        gamma=C(np.asarray(L0_attn_gamma, dtype=np.float32)),
        beta=C(np.asarray(L0_attn_beta, dtype=np.float32)),
        s_g=C(np.asarray(L0_ffn_s_gate, dtype=np.float32)),
        s_u=C(np.asarray(L0_ffn_s_up, dtype=np.float32)),
        s_d=C(np.asarray(L0_ffn_s_down, dtype=np.float32)),
        ropc=ropc_np, rops=rops_np, ident=ident,
    )
    X = np.asarray(L0_attn_X, dtype=np.float32)
    nc = build_program()
    in_maps = []
    for c in range(8):
        m = dict(shared)
        m["x"] = np.ascontiguousarray(X[c])
        in_maps.append(m)
    res = run_bass_kernel_spmd(nc, in_maps, core_ids=list(range(8)))
    return np.stack([res.results[c]["out"] for c in range(8)], axis=0)


if __name__ == "__main__":
    nc = build_program()
    print("built ok, instructions:", nc.n_instructions())


# version 28
# speedup vs baseline: 1.1700x; 1.1700x over previous
import numpy as np
import ml_dtypes
import concourse.bass as bass
import concourse.mybir as mybir
from concourse.bass_utils import run_bass_kernel_spmd

F32 = mybir.dt.float32
BF16 = mybir.dt.bfloat16
I32 = mybir.dt.int32
AF = mybir.ActivationFunctionType
ALU = mybir.AluOpType

S = 2048
D = 2048
HID = 5632
NKT = D // 128
NHT = HID // 128
NT = S // 128
ROT = 1024
EPS = 1e-5
KB = 1024


class Sem:
    def __init__(self, nc, name):
        self.h = nc.alloc_semaphore(name)
        self.v = 0

    def inc(self, by=1):
        self.v += by
        return self.v


def _flat(waits):
    out = []
    for w in waits:
        if w is None:
            continue
        if isinstance(w, list):
            out.extend(x for x in w if x is not None)
        else:
            out.append(w)
    return out


def build_program():
    nc = bass.Bass("TRN2", target_bir_lowering=False)

    x_in = nc.dram_tensor("x", [S, D], F32, kind="ExternalInput").ap()
    wq4 = nc.dram_tensor("wq4", [NKT, 128, NKT, 128], F32, kind="ExternalInput").ap()
    wk4 = nc.dram_tensor("wk4", [4, 128, NKT, 512], F32, kind="ExternalInput").ap()
    wv4 = nc.dram_tensor("wv4", [4, 128, NKT, 512], F32, kind="ExternalInput").ap()
    b_in = nc.dram_tensor("b_in", [3 * D], F32, kind="ExternalInput").ap()
    wo4 = nc.dram_tensor("wo4", [4, 128, NKT, 512], F32, kind="ExternalInput").ap()
    b_out = nc.dram_tensor("b_out", [D], F32, kind="ExternalInput").ap()
    gamma = nc.dram_tensor("gamma", [D], F32, kind="ExternalInput").ap()
    beta = nc.dram_tensor("beta", [D], F32, kind="ExternalInput").ap()
    wg4 = nc.dram_tensor("wg4", [NHT, 128, NKT, 128], I32, kind="ExternalInput").ap()
    wu4 = nc.dram_tensor("wu4", [NHT, 128, NKT, 128], I32, kind="ExternalInput").ap()
    wd4 = nc.dram_tensor("wd4", [4, 128, NHT, 512], I32, kind="ExternalInput").ap()
    s_g = nc.dram_tensor("s_g", [HID], F32, kind="ExternalInput").ap()
    s_u = nc.dram_tensor("s_u", [HID], F32, kind="ExternalInput").ap()
    s_d = nc.dram_tensor("s_d", [D], F32, kind="ExternalInput").ap()
    ropc = nc.dram_tensor("ropc", [S, ROT // 2], F32, kind="ExternalInput").ap()
    rops = nc.dram_tensor("rops", [S, ROT // 2], F32, kind="ExternalInput").ap()
    ident_in = nc.dram_tensor("ident", [128, 128], F32, kind="ExternalInput").ap()

    out = nc.dram_tensor("out", [S, D], F32, kind="ExternalOutput").ap()

    qts = nc.dram_tensor("qts", [NKT, 128, S], BF16, kind="Internal").ap()
    xnts = nc.dram_tensor("xnts", [NKT, 128, S], BF16, kind="Internal").ap()
    wgs4 = nc.dram_tensor("wgs4", [NHT, 128, NKT, 128], BF16, kind="Internal").ap()
    wus4 = nc.dram_tensor("wus4", [NHT, 128, NKT, 128], BF16, kind="Internal").ap()
    wds4 = nc.dram_tensor("wds4", [4, 128, NHT, 512], BF16, kind="Internal").ap()
    a_scr = nc.dram_tensor("a_scr", [1, D], F32, kind="Internal").ap()

    def sb_at(name, shape, dtype, at):
        h = nc.alloc_sbuf_tensor_at(name, shape, dtype, offset=at)
        per_part = int(np.prod(shape[1:])) * mybir.dt.size(dtype)
        return h.ap(), at + per_part

    class Region:
        def __init__(self, start):
            self.ptr = start

        def al(self, name, shape, dtype):
            at = (self.ptr + 31) // 32 * 32
            ap, end = sb_at(name, shape, dtype, at)
            self.ptr = end
            return ap

    SBUF_BASE = 16512
    LIMIT = 229344
    G = Region(SBUF_BASE)
    idtf = G.al("idtf", [128, 128], F32)
    idtb = G.al("idtb", [128, 128], BF16)
    binP = G.al("binP", [128, 48], F32)
    sgP = G.al("sgP", [128, NHT], F32)
    suP = G.al("suP", [128, NHT], F32)
    gamT = G.al("gamT", [128, NKT], F32)
    betT = G.al("betT", [128, NKT], F32)
    mnegA = G.al("mnegA", [128, NT], F32)
    rstdA = G.al("rstdA", [128, NT], F32)
    bias2A = G.al("bias2A", [128, NT], F32)
    r2A = G.al("r2A", [128, NT], F32)
    ssqKA = G.al("ssqKA", [128, 8], F32)
    ssqVA = G.al("ssqVA", [128, 8], F32)
    wvec = G.al("wvec", [128, 8], BF16)
    tmpA = G.al("tmpA", [128, 8], F32)
    tmp1A = G.al("tmp1A", [128, 4], F32)
    epsP = G.al("epsP", [128, 1], F32)
    a_T = G.al("a_T", [128, NKT], F32)
    bnst = [G.al("bnst0", [128, 24], F32), G.al("bnst1", [128, 24], F32)]
    mvst = [G.al("mvst0", [128, 2], F32), G.al("mvst1", [128, 2], F32)]
    GLOBAL_END = G.ptr

    P = Region(GLOBAL_END)
    xn2T = P.al("xn2T", [128, NKT, KB], BF16)
    pbuf = P.al("pbuf", [128, 8, D], BF16)
    binBK = P.al("binBK", [128, D], F32)
    binBV = P.al("binBV", [128, D], F32)
    _xoff = (P.ptr + 31) // 32 * 32
    x_t = [P.al("x_t0", [128, D], F32), P.al("x_t1", [128, D], F32)]
    _xnoff = (P.ptr + 31) // 32 * 32
    xn_t = [P.al("xn_t0", [128, D], F32), P.al("xn_t1", [128, D], F32)]
    xnr2b = [P.al(f"xnr2b{i}", [128, D], BF16) for i in range(4)]
    kvbK2, _ = sb_at("kvbK2", [128, NKT, 512], BF16, _xoff)
    kvbV2, _ = sb_at("kvbV2", [128, NKT, 512], BF16, _xnoff)
    xntg = P.al("xntg", [128, NKT, 256], BF16)
    a_row = P.al("a_row", [1, D], F32)
    kvbK = P.al("kvbK", [128, NKT, 512], BF16)
    kvbV = P.al("kvbV", [128, NKT, 512], BF16)
    wqB = [P.al("wqB0", [128, NKT, 128], BF16), P.al("wqB1", [128, NKT, 128], BF16)]
    qtstg = [P.al("qtstg0", [128, 512], BF16), P.al("qtstg1", [128, 512], BF16)]
    krst = [P.al("krst0", [128, 512], F32), P.al("krst1", [128, 512], F32)]
    vrst = [P.al("vrst0", [128, 512], F32), P.al("vrst1", [128, 512], F32)]
    sqscr = P.al("sqscr", [128, 512], F32)
    vscr = P.al("vscr", [128, 512], F32)
    ropcB = [P.al("ropcB0", [128, 256], F32), P.al("ropcB1", [128, 256], F32)]
    ropsB = [P.al("ropsB0", [128, 256], F32), P.al("ropsB1", [128, 256], F32)]
    assert P.ptr <= LIMIT, f"P1 over SBUF: {P.ptr}"

    F = Region(GLOBAL_END)
    hT = F.al("hT", [128, NHT, KB], BF16)
    HT_END = F.ptr
    xnT = F.al("xnT", [128, NKT, KB], BF16)
    wgB = [F.al("wgB0", [128, NKT, 128], BF16), F.al("wgB1", [128, NKT, 128], BF16)]
    wuB = [F.al("wuB0", [128, NKT, 128], BF16), F.al("wuB1", [128, NKT, 128], BF16)]
    sigstg = F.al("sigstg", [128, 512], F32)
    upstg = F.al("upstg", [128, 512], F32)
    assert F.ptr <= LIMIT, f"F1 over SBUF: {F.ptr}"

    F2 = Region(HT_END)
    wdB = F2.al("wdB", [128, NHT, 512], BF16)
    woB = F2.al("woB", [128, NKT, 512], BF16)
    qtg = F2.al("qtg", [128, NKT, 512], BF16)
    b_outB = F2.al("b_outB", [128, D], F32)
    s_dB = F2.al("s_dB", [128, D], F32)
    x_ij = [F2.al("x_ij0", [128, 512], F32), F2.al("x_ij1", [128, 512], F32)]
    ffnstg = F2.al("ffnstg", [128, 512], F32)
    gstg = F2.al("gstg", [128, 512], F32)
    xnstg = F2.al("xnstg", [128, 512], F32)
    s1stg = F2.al("s1stg", [128, 512], F32)
    ostg = [F2.al("ostg0", [128, 512], F32), F2.al("ostg1", [128, 512], F32)]
    assert F2.ptr <= LIMIT, f"F2C over SBUF: {F2.ptr}"

    psX = [nc.alloc_psum_tensor(f"psX{i}", [128, 512], F32).ap() for i in range(4)]
    psT = [nc.alloc_psum_tensor(f"psT{i}", [128, 512], F32).ap() for i in range(2)]
    psA = nc.alloc_psum_tensor("psA", [128, 512], F32).ap()

    dmaI = Sem(nc, "dmaI")
    xldS = [Sem(nc, "xld0"), Sem(nc, "xld1")]
    ropeS = [Sem(nc, "rope0"), Sem(nc, "rope1")]
    sxnS = Sem(nc, "sxn")
    wqS = [Sem(nc, "wq0"), Sem(nc, "wq1")]
    sqtS = [Sem(nc, "sqt0"), Sem(nc, "sqt1")]
    kvS = [Sem(nc, "kv0"), Sem(nc, "kv1")]
    saS = Sem(nc, "saS")
    aTS = Sem(nc, "aTS")
    bsS = Sem(nc, "bsS")
    xnlS = Sem(nc, "xnlS")
    wguS = [Sem(nc, "wgu0"), Sem(nc, "wgu1")]
    wglS = [Sem(nc, "wgl0"), Sem(nc, "wgl1")]
    swgS = [Sem(nc, "swg0"), Sem(nc, "swg1")]
    wdlS = [Sem(nc, f"wdl{s}") for s in range(4)]
    wdl2S = [Sem(nc, f"wdm{s}") for s in range(4)]
    swdS = [Sem(nc, f"swd{s}") for s in range(4)]
    woS = Sem(nc, "woS")
    qtlS = Sem(nc, "qtlS")
    xijS = [Sem(nc, "xij0"), Sem(nc, "xij1")]
    ibS = Sem(nc, "ibS")
    outS = [Sem(nc, "out0"), Sem(nc, "out1")]
    semT = Sem(nc, "semT")
    semV = Sem(nc, "semV")
    semS = Sem(nc, "semS")
    semG = Sem(nc, "semG")
    semW = Sem(nc, "semW")
    wctr = [0]

    def vg(e, ins):
        wctr[0] += 1
        ins.then_inc(semW.h, 1)
        e.wait_ge(semW.h, wctr[0])

    semY = Sem(nc, "semY")
    yctr = [0]

    def sg(e, ins):
        yctr[0] += 1
        ins.then_inc(semY.h, 1)
        e.wait_ge(semY.h, yctr[0])

    plan = {"sync": [], "gpsimd": [], "vector": [], "scalar": [], "tensor": []}

    def dma(q, out_ap, in_ap, sem, waits=(), slow=False):
        wl = _flat(waits)

        def f(e, wl=wl, out_ap=out_ap, in_ap=in_ap, sem=sem, slow=slow):
            for (s, v) in wl:
                if v > 0:
                    e.wait_ge(s.h, v)
            e.dma_start(out_ap, in_ap,
                        allow_slow_non_contiguous=slow).then_inc(sem.h, 16)

        plan[q].append(f)
        sem.inc(16)
        return (sem, sem.v)

    eng_chain = {"vector": semV, "scalar": semS, "gpsimd": semG}

    def oneng(eng, fn, sem=None, waits=()):
        ch = eng_chain.get(eng)
        if ch is not None:
            assert sem is None or sem is ch
            sem = ch
        wl = _flat(list(waits) + ([(ch, ch.v)] if ch is not None else []))
        byid = {}
        for (s, v) in wl:
            if v > byid.get(id(s), (None, 0))[1]:
                byid[id(s)] = (s, v)
        wl = list(byid.values())
        if sem is not None:
            sem.inc(1)

        def f(e, wl=wl, fn=fn, sem=sem):
            for (s, v) in wl:
                if v > 0:
                    e.wait_ge(s.h, v)
            ins = fn(e)
            if sem is not None:
                assert ins is not None, "producer fn must return its last instr"
                ins.then_inc(sem.h, 1)

        plan[eng].append(f)
        return (sem, sem.v) if sem is not None else None

    bank_free = {}
    bank_ready = {}

    def mm_group(ps, mms, waits=(), start=True, stop=True):
        wl = _flat(list(waits) + ([bank_free.get(id(ps))] if start else []))

        def f(e, wl=wl, mms=mms, start=start, stop=stop):
            for (s, v) in wl:
                if v > 0:
                    e.wait_ge(s.h, v)
            n = len(mms)
            ins = None
            for i, (l, r, o) in enumerate(mms):
                ins = e.matmul(o if o is not None else ps, l, r,
                               start=(start and i == 0),
                               stop=(stop and i == n - 1))
            ins.then_inc(semT.h, 1)

        plan["tensor"].append(f)
        semT.inc(1)
        evt = (semT, semT.v)
        bank_ready[id(ps)] = evt
        return evt

    def pe_warm(waits=()):
        wl = _flat(waits)

        def f(e, wl=wl):
            for (s, v) in wl:
                if v > 0:
                    e.wait_ge(s.h, v)
            e.transpose(psA[:, 0:128], idtf, idtf)

        plan["tensor"].append(f)

    def tp_group(ps, items, waits=()):
        wl = _flat(list(waits) + [bank_free.get(id(ps))])

        def f(e, wl=wl, items=items):
            for (s, v) in wl:
                if v > 0:
                    e.wait_ge(s.h, v)
            ins = None
            for (src, o) in items:
                ins = e.transpose(o, src, idtf)
            ins.then_inc(semT.h, 1)

        plan["tensor"].append(f)
        semT.inc(1)
        evt = (semT, semT.v)
        bank_ready[id(ps)] = evt
        return evt

    def evac(eng_name, ps, fn, sem, waits=()):
        evt = oneng(eng_name, fn, sem=sem,
                    waits=list(waits) + [bank_ready[id(ps)]])
        bank_free[id(ps)] = evt
        return evt

    dma("sync", idtf, ident_in, dmaI)
    idtb_done = dma("gpsimd", idtb, ident_in, ibS)
    dma("sync", binP, b_in.rearrange("(m p) -> p m", p=128), dmaI, slow=True)
    dma("sync", sgP, s_g.rearrange("(m p) -> p m", p=128), dmaI, slow=True)
    dma("sync", suP, s_u.rearrange("(m p) -> p m", p=128), dmaI, slow=True)
    dma("sync", gamT, gamma.rearrange("(k p) -> p k", p=128), dmaI, slow=True)
    dma("sync", betT, beta.rearrange("(k p) -> p k", p=128), dmaI, slow=True)
    dma("sync", binBK, b_in[D:2 * D].rearrange("(o d) -> o d", o=1).broadcast_to([128, D]), dmaI)
    init_done = dma("sync", binBV,
                    b_in[2 * D:3 * D].rearrange("(o d) -> o d", o=1).broadcast_to([128, D]),
                    dmaI)
    a_zero_evt = oneng("vector", lambda e: e.memset(a_row, 0.0), sem=semV,
                       waits=[init_done])
    eps_evt = oneng("vector", lambda e: e.memset(epsP, EPS), sem=semV)
    def load_set6(e):
        return e.add_instruction(mybir.InstLoadActFuncSet(
            name=nc.get_next_instruction_name(), act_func_set_id=6,
            ins=[], outs=[]))
    oneng("scalar", load_set6, waits=[init_done])

    x_cons = [None, None]
    xn_cons = [None, None]
    xr2b_evt = [None] * 4
    xr2b_cons = [None]
    bnst_free = [None, None]
    xntg_free = [None]
    pend_spill = [None]
    xn2T_guard = [None]
    kv2_guard = [None]

    def p1_stepA_tile(h, il):
        gi = h * 8 + il
        par = gi % 2
        sl = il % 2
        quad = il % 4
        tb = par * 4
        lnc = tmpA[:, tb + 0:tb + 1]
        t2 = tmpA[:, tb + 1:tb + 2]

        alias_g = [kv2_guard[0]] if (h == 1 and il < 2) else []
        x_evt = dma("sync", x_t[par], x_in[gi * 128:(gi + 1) * 128, :],
                    xldS[par], waits=[x_cons[par]] + alias_g)
        if pend_spill[0] is not None:
            g2d, evd = pend_spill[0]
            pend_spill[0] = None
            last = None
            for part in range(4):
                last = dma("sync",
                           xnts[part * 4:(part + 1) * 4, :,
                                g2d * 256:(g2d + 1) * 256]
                           .rearrange("k p t -> p k t"),
                           xntg[:, part * 4:(part + 1) * 4, :],
                           sxnS, waits=evd if part == 0 else [])
            xntg_free[0] = last

        def bnf(e, par=par):
            ins = None
            for c in range(4):
                ins = e.bn_stats(bnst[par][:, c * 6:(c + 1) * 6],
                                 x_t[par][:, c * 512:(c + 1) * 512])
                if c < 3:
                    vg(e, ins)
            return ins
        bn_done = oneng("vector", bnf, sem=semV,
                        waits=[x_evt, bnst_free[par]])
        agg_done = oneng("vector",
                         lambda e, par=par: e.bn_aggr(mvst[par], bnst[par]),
                         sem=semV, waits=[bn_done])
        bnst_free[par] = agg_done

        ev1 = oneng("vector",
                    lambda e, par=par, gi=gi: e.tensor_scalar(
                        mnegA[:, gi:gi + 1], mvst[par][:, 0:1],
                        -1.0, None, ALU.mult),
                    sem=semV, waits=[agg_done])

        def r1f(e, par=par, gi=gi, lnc=lnc):
            sg(e, e.activation(lnc, mvst[par][:, 1:2], AF.Ln, bias=epsP))
            return e.activation(rstdA[:, gi:gi + 1], lnc, AF.Exp, scale=-0.5)
        sc1 = oneng("scalar", r1f, sem=semS, waits=[agg_done, eps_evt])

        def fn2(e, par=par, gi=gi, t2=t2):
            vg(e, e.tensor_tensor(bias2A[:, gi:gi + 1], mnegA[:, gi:gi + 1],
                                  rstdA[:, gi:gi + 1], ALU.mult))
            vg(e, e.tensor_tensor(t2, mvst[par][:, 1:2],
                                  rstdA[:, gi:gi + 1], ALU.mult))
            return e.tensor_tensor(t2, t2, rstdA[:, gi:gi + 1], ALU.mult)
        ev2 = oneng("vector", fn2, sem=semV, waits=[sc1, ev1])

        def r2f(e, gi=gi, lnc=lnc, t2=t2):
            sg(e, e.activation(lnc, t2, AF.Ln, bias=epsP))
            return e.activation(r2A[:, gi:gi + 1], lnc, AF.Exp, scale=-0.5)
        sc2 = oneng("scalar", r2f, sem=semS, waits=[ev2])

        pe_warm(waits=[bank_free.get(id(psA)), bn_done])
        pe_warm(waits=[ev2])

        xn_evt = oneng("gpsimd",
                       lambda e, par=par, gi=gi: e.tensor_scalar(
                           xn_t[par], x_t[par], mnegA[:, gi:gi + 1],
                           rstdA[:, gi:gi + 1], ALU.add, ALU.mult),
                       sem=semG, waits=[x_evt, ev1, sc1, xn_cons[par]] + alias_g)
        x_cons[par] = [bn_done, xn_evt]

        xr2_evt = oneng("scalar",
                        lambda e, par=par, gi=gi, quad=quad: e.activation(
                            xnr2b[quad], xn_t[par], AF.Identity,
                            scale=r2A[:, gi:gi + 1]),
                        sem=semS, waits=[xn_evt, sc2, xr2b_cons[0]])
        xr2b_evt[quad] = xr2_evt

        last_tp = None
        for g in range(4):
            ps = psT[g % 2]
            items = [(xn_t[par][:, (4 * g + q) * 128:(4 * g + q + 1) * 128],
                      ps[:, q * 128:(q + 1) * 128]) for q in range(4)]
            last_tp = tp_group(ps, items, waits=[xn_evt, init_done])
            ew = [xntg_free[0]] if (sl == 0 and g == 0) else []

            def ev(e, ps=ps, g=g, sl=sl):
                return e.tensor_copy(
                    xntg[:, 4 * g:4 * (g + 1), sl * 128:(sl + 1) * 128],
                    ps.rearrange("p (q t) -> p q t", q=4))
            evac("vector", ps, ev, semV, waits=ew)
        xn_cons[par] = [xr2_evt, last_tp]

        if sl == 1:
            pend_spill[0] = (gi // 2, [(semV, semV.v), (semS, semS.v)])

        if quad == 3:
            a4 = il // 4
            last_tp2 = None
            for kt in range(NKT):
                ps = psX[kt % 4]
                mms = [(xnr2b[q][:, kt * 128:(kt + 1) * 128], idtb,
                        ps[:, q * 128:(q + 1) * 128]) for q in range(4)]
                w0 = list(xr2b_evt) + [idtb_done]
                last_tp2 = mm_group(ps, mms, waits=w0)
                gw = [xn2T_guard[0]] if (h == 1 and a4 == 0 and kt < 4) else []

                def fxs(e, ps=ps, kt=kt, a4=a4):
                    return e.activation(
                        xn2T[:, kt, a4 * 512:(a4 + 1) * 512], ps, AF.Identity,
                        scale=gamT[:, kt:kt + 1], bias=betT[:, kt:kt + 1])
                evac("scalar", ps, fxs, semS, waits=gw)
            xr2b_cons[0] = last_tp2

    wq_cons = [None, None]
    stg_free = [None, None]

    def p1_stepB(h, a_done):
        for m in range(NKT):
            sl = m % 2
            w_evt = dma("gpsimd", wqB[sl], wq4[m],
                        wqS[sl],
                        waits=[wq_cons[sl]] + ([init_done] if m == 0 else []))
            last_grp = None
            for n in range(2):
                ps = psX[(2 * m + n) % 4]
                mms = [(wqB[sl][:, k, :], xn2T[:, k, n * 512:(n + 1) * 512], None)
                       for k in range(NKT)]
                extra = a_done if (m == 0 and n == 0) else []
                last_grp = mm_group(ps, mms, waits=[w_evt] + extra)
                st = qtstg[n]

                def ev(e, ps=ps, st=st, m=m):
                    return e.activation(st, ps, AF.Identity,
                                        bias=binP[:, m:m + 1])
                vv = evac("scalar", ps, ev, semS, waits=[stg_free[n]])
                stg_free[n] = dma("sync",
                                  qts[m, :, h * KB + n * 512:h * KB + (n + 1) * 512],
                                  st, sqtS[n], waits=[vv])
            wq_cons[sl] = last_grp

    kv_cons = [None, None]
    rope_cons = [None, None]

    def p1_stepC(h, a_eng_done):
        k_free = [None, None]
        v_free = [None, None]
        pbuf_guard = (semT, semT.v)
        z1 = oneng("vector", lambda e: e.memset(ssqKA, 0.0), sem=semV,
                   waits=[pbuf_guard, init_done])
        z2 = oneng("vector", lambda e: e.memset(ssqVA, 0.0), sem=semV)
        zevt = z2
        for c in range(4):
            slot = c % 2
            bufK = kvbK if slot == 0 else kvbK2
            bufV = kvbV if slot == 0 else kvbV2
            w0 = [kv_cons[slot]]
            if c == 1:
                w0 = w0 + a_eng_done
            dma("gpsimd", bufK, wk4[c], kvS[slot], waits=w0)
            kv_evt = dma("gpsimd", bufV, wv4[c], kvS[slot], waits=[])
            for il in range(8):
                gi = h * 8 + il
                rsl = il % 2
                kb = krst[rsl]
                vb = vrst[rsl]
                rope_evt = None
                if c < 2:
                    dma("sync", ropcB[rsl],
                        ropc[gi * 128:(gi + 1) * 128, c * 256:(c + 1) * 256],
                        ropeS[rsl], waits=[rope_cons[rsl]])
                    rope_evt = dma("sync", ropsB[rsl],
                                   rops[gi * 128:(gi + 1) * 128,
                                        c * 256:(c + 1) * 256],
                                   ropeS[rsl], waits=[])
                psK = psX[(2 * il) % 4]
                psV = psX[(2 * il + 1) % 4]
                mm_group(psK, [(xn2T[:, k, il * 128:(il + 1) * 128],
                                bufK[:, k, :], None) for k in range(NKT)],
                         waits=[kv_evt])
                mm_group(psV, [(xn2T[:, k, il * 128:(il + 1) * 128],
                                bufV[:, k, :], None) for k in range(NKT)],
                         waits=[kv_evt])
                if il == 7:
                    kv_cons[slot] = bank_ready[id(psV)]

                def kpath(e, psK=psK, c=c, rsl=rsl, kb=kb):
                    ins = e.tensor_tensor(kb, psK,
                                          binBK[:, c * 512:(c + 1) * 512], ALU.add)
                    if c < 2:
                        vg(e, ins)
                        cs, sn = ropcB[rsl], ropsB[rsl]
                        ke = kb[:, 0:512:2]
                        ko = kb[:, 1:512:2]
                        vg(e, e.tensor_tensor(vscr[:, 0:256], ke, sn, ALU.mult))
                        vg(e, e.tensor_tensor(vscr[:, 256:512], ko, sn, ALU.mult))
                        vg(e, e.tensor_tensor(ke, ke, cs, ALU.mult))
                        vg(e, e.tensor_tensor(ko, ko, cs, ALU.mult))
                        vg(e, e.tensor_tensor(ke, ke, vscr[:, 256:512], ALU.subtract))
                        ins = e.tensor_tensor(ko, ko, vscr[:, 0:256], ALU.add)
                    return ins
                kw = [bank_ready[id(psK)], zevt, k_free[rsl]]
                if c < 2:
                    kw.append(rope_evt)
                vvK = oneng("vector", kpath, sem=semV, waits=kw)
                bank_free[id(psK)] = vvK
                sS = oneng("scalar",
                           lambda e, kb=kb: e.activation(
                               sqscr, kb, AF.Square, accum_out=tmp1A[:, 0:1]),
                           sem=semS, waits=[vvK])
                k_free[rsl] = sS
                vv2 = oneng("vector",
                            lambda e, il=il: e.tensor_tensor(
                                ssqKA[:, il:il + 1], ssqKA[:, il:il + 1],
                                tmp1A[:, 0:1], ALU.add),
                            sem=semV, waits=[sS])

                def vpath(e, psV=psV, c=c, rsl=rsl, vb=vb):
                    ins = e.tensor_tensor(vb, psV,
                                          binBV[:, c * 512:(c + 1) * 512], ALU.add)
                    if c < 2:
                        vg(e, ins)
                        cs, sn = ropcB[rsl], ropsB[rsl]
                        ve_ = vb[:, 0:512:2]
                        vo = vb[:, 1:512:2]
                        vg(e, e.tensor_tensor(vscr[:, 0:256], ve_, sn, ALU.mult))
                        vg(e, e.tensor_tensor(vscr[:, 256:512], vo, sn, ALU.mult))
                        vg(e, e.tensor_tensor(ve_, ve_, cs, ALU.mult))
                        vg(e, e.tensor_tensor(vo, vo, cs, ALU.mult))
                        vg(e, e.tensor_tensor(ve_, ve_, vscr[:, 256:512], ALU.subtract))
                        ins = e.tensor_tensor(vo, vo, vscr[:, 0:256], ALU.add)
                    return ins
                vw = [bank_ready[id(psV)], v_free[rsl]]
                if c < 2:
                    vw.append(rope_evt)
                vvV = oneng("vector", vpath, sem=semV, waits=vw)
                bank_free[id(psV)] = vvV
                if c < 2:
                    rope_cons[rsl] = vvV
                sS2 = oneng("scalar",
                            lambda e, vb=vb: e.activation(
                                sqscr, vb, AF.Square, accum_out=tmp1A[:, 1:2]),
                            sem=semS, waits=[vvV])
                vv4 = oneng("vector",
                            lambda e, il=il: e.tensor_tensor(
                                ssqVA[:, il:il + 1], ssqVA[:, il:il + 1],
                                tmp1A[:, 1:2], ALU.add),
                            sem=semV, waits=[sS2])
                pv = oneng("vector",
                           lambda e, il=il, c=c, kb=kb, vb=vb: e.tensor_tensor(
                               pbuf[:, il, c * 512:(c + 1) * 512], kb, vb,
                               ALU.mult),
                           sem=semV, waits=[vv2, vv4])
                v_free[rsl] = [sS2, pv]

        wlast = [None, None]
        for il in range(8):
            p2 = il % 2
            vv = oneng("vector",
                       lambda e, il=il, p2=p2: e.tensor_tensor(
                           tmp1A[:, p2:p2 + 1], ssqKA[:, il:il + 1],
                           ssqVA[:, il:il + 1], ALU.mult),
                       sem=semV, waits=[wlast[p2]])

            def wf(e, il=il, p2=p2):
                sg(e, e.activation(tmp1A[:, 2 + p2:3 + p2],
                                   tmp1A[:, p2:p2 + 1], AF.Ln))
                return e.activation(wvec[:, il:il + 1],
                                    tmp1A[:, 2 + p2:3 + p2], AF.Exp,
                                    scale=-0.5)
            wlast[p2] = oneng("scalar", wf, sem=semS, waits=[vv])
        w_ready = [(semS, semS.v)]
        for c2 in range(4):
            mms = [(wvec[:, il:il + 1], pbuf[:, il, c2 * 512:(c2 + 1) * 512],
                    psA[0:1, :]) for il in range(8)]
            mm_group(psA, mms, waits=w_ready + [a_zero_evt])

            def aev(e, c2=c2):
                return e.tensor_tensor(a_row[0:1, c2 * 512:(c2 + 1) * 512],
                                       a_row[0:1, c2 * 512:(c2 + 1) * 512],
                                       psA[0:1, :], ALU.add)
            evac("vector", psA, aev, semV)

    for h in range(2):
        for il in range(8):
            p1_stepA_tile(h, il)
        if pend_spill[0] is not None:
            g2d, evd = pend_spill[0]
            pend_spill[0] = None
            last = None
            for part in range(4):
                last = dma("sync",
                           xnts[part * 4:(part + 1) * 4, :,
                                g2d * 256:(g2d + 1) * 256]
                           .rearrange("k p t -> p k t"),
                           xntg[:, part * 4:(part + 1) * 4, :],
                           sxnS, waits=evd if part == 0 else [])
            xntg_free[0] = last
        a_done = [(semS, semS.v), (semV, semV.v)]
        a_eng_done = [(semS, semS.v), (semV, semV.v), (semT, semT.v),
                      (semG, semG.v)]
        p1_stepB(h, a_done)
        p1_stepC(h, a_eng_done)
        kv2_guard[0] = kv_cons[1]
        xn2T_guard[0] = (semT, semT.v)

    vv = oneng("vector",
               lambda e: e.tensor_scalar(a_row, a_row, 0.5, 0.5,
                                         ALU.mult, ALU.add),
               sem=semV)
    ase = dma("sync", a_scr, a_row, saS, waits=[vv])
    aT_ready = dma("sync", a_T, a_scr.rearrange("o (k p) -> p (o k)", p=128),
                   aTS, waits=[ase], slow=True)

    p1_all_done = [(semV, semV.v), (semS, semS.v), (semT, semT.v),
                   (semG, semG.v), (sxnS, sxnS.v),
                   (sqtS[0], sqtS[0].v), (sqtS[1], sqtS[1].v)]
    dma("sync", b_outB,
        b_out.rearrange("(o d) -> o d", o=1).broadcast_to([128, D]), bsS,
        waits=p1_all_done)
    bs_loaded = dma("sync", s_dB,
                    s_d.rearrange("(o d) -> o d", o=1).broadcast_to([128, D]),
                    bsS)

    wf_cons = [None, None]
    wf_spill = [None, None]
    hs_free = [None]

    def f1(h):
        if h == 0:
            oneng("scalar", lambda e: e.preload_act_table(AF.Silu))
        f2c_guard = ([(semT, semT.v)] +
                     [(s, s.v) for s in swdS]) if h == 1 else []
        xnT_evt = None
        for part in range(4):
            w0 = []
            if part == 0:
                w0 = p1_all_done if h == 0 else f2c_guard
            xnT_evt = dma("sync", xnT[:, part * 4:(part + 1) * 4, :],
                          xnts[part * 4:(part + 1) * 4, :, h * KB:(h + 1) * KB]
                          .rearrange("k p t -> p k t"),
                          xnlS, waits=w0)
        for m in range(NHT):
            sl = m % 2
            reuse = [wf_cons[sl], wf_spill[sl]]
            if h == 0:
                dma("gpsimd", wgB[sl], wg4[m], wguS[sl],
                    waits=[xnT_evt] + reuse)
                w_evt = dma("gpsimd", wuB[sl], wu4[m], wguS[sl], waits=[])

                def sub128(e, sl=sl):
                    vg(e, e.tensor_scalar(wgB[sl].rearrange("p k c -> p (k c)"),
                                          wgB[sl].rearrange("p k c -> p (k c)"),
                                          -128.0, None, ALU.add))
                    return e.tensor_scalar(wuB[sl].rearrange("p k c -> p (k c)"),
                                           wuB[sl].rearrange("p k c -> p (k c)"),
                                           -128.0, None, ALU.add)
                w_evt = oneng("vector", sub128, sem=semV, waits=[w_evt])
                dma("sync", wgs4[m], wgB[sl], swgS[sl], waits=[w_evt])
                wf_spill[sl] = dma("sync", wus4[m], wuB[sl], swgS[sl],
                                   waits=[])
            else:
                dma("sync", wgB[sl], wgs4[m], wglS[sl],
                    waits=([xnT_evt] + reuse))
                w_evt = dma("sync", wuB[sl], wus4[m], wglS[sl], waits=[])
                wf_spill[sl] = None
            lastg = None
            for n in range(2):
                psG = psX[(2 * m + n) % 2]
                psU = psX[2 + (2 * m + n) % 2]
                mm_group(psG, [(wgB[sl][:, k, :],
                                xnT[:, k, n * 512:(n + 1) * 512], None)
                               for k in range(NKT)], waits=[w_evt])
                lastg = mm_group(psU, [(wuB[sl][:, k, :],
                                        xnT[:, k, n * 512:(n + 1) * 512], None)
                                       for k in range(NKT)], waits=[w_evt])
                sSe = evac("scalar", psG,
                           lambda e, psG=psG, m=m: e.activation(
                               sigstg, psG, AF.Silu, scale=sgP[:, m:m + 1]),
                           semS, waits=[hs_free[0]])
                vVe = evac("vector", psU,
                           lambda e, psU=psU, m=m: e.tensor_scalar(
                               upstg, psU, suP[:, m:m + 1], None, ALU.mult),
                           semV, waits=[hs_free[0]])
                vv2 = oneng("vector",
                            lambda e, m=m, n=n: e.tensor_tensor(
                                hT[:, m, n * 512:(n + 1) * 512], sigstg, upstg,
                                ALU.mult),
                            sem=semV, waits=[sSe, vVe])
                hs_free[0] = vv2
                bank_free[id(psG)] = vv2
                bank_free[id(psU)] = vv2
            wf_cons[sl] = lastg

    wd_seg_cons = [None] * 4
    wd_seg_spill = [None] * 4
    wo_cons = [None]
    qt_cons = [None]
    xij_free = [None, None]
    gs_free = [None]
    cmb_free = [None]
    SEGS = [(11 * s, [(11 * s, 4), (11 * s + 4, 4), (11 * s + 8, 3)])
            for s in range(4)]

    def f2c(h):
        f1_done = [(semV, semV.v), (semT, semT.v)]
        if h == 0:
            f1_done += [(swgS[0], swgS[0].v), (swgS[1], swgS[1].v)]
        for j in range(4):
            wd_seg_evt = [None] * 4
            for s, (base, parts) in enumerate(SEGS):
                w0 = [wd_seg_cons[s], wd_seg_spill[s]]
                if j == 0:
                    w0 = w0 + f1_done
                if h == 0:
                    ld = dma("gpsimd", wdB[:, base:base + 11, :],
                             wd4[j, :, base:base + 11, :],
                             wdlS[s], waits=w0)
                    sub = oneng("vector",
                                lambda e, base=base: e.tensor_scalar(
                                    wdB[:, base:base + 11, :]
                                    .rearrange("p k c -> p (k c)"),
                                    wdB[:, base:base + 11, :]
                                    .rearrange("p k c -> p (k c)"),
                                    -128.0, None, ALU.add),
                                sem=semV, waits=[ld])
                    wd_seg_spill[s] = dma("sync",
                                          wds4[j, :, base:base + 11, :],
                                          wdB[:, base:base + 11, :],
                                          swdS[s], waits=[sub])
                    wd_seg_evt[s] = sub
                else:
                    ld = dma("sync", wdB[:, base:base + 11, :],
                             wds4[j, :, base:base + 11, :],
                             wdl2S[s], waits=w0)
                    wd_seg_spill[s] = None
                    wd_seg_evt[s] = ld
            wo_evt = dma("gpsimd", woB, wo4[j], woS,
                         waits=f1_done + [wo_cons[0]])

            last_bank = None
            for g in range(2):
                qt_evt = None
                for part in range(4):
                    qt_evt = dma("sync", qtg[:, part * 4:(part + 1) * 4, :],
                                 qts[part * 4:(part + 1) * 4, :,
                                     h * KB + g * 512:h * KB + (g + 1) * 512]
                                 .rearrange("k p t -> p k t"),
                                 qtlS,
                                 waits=([aT_ready, qt_cons[0]] + f1_done)
                                 if part == 0 else [])

                def ascale(e):
                    ins = None
                    for k in range(NKT):
                        ins = e.tensor_scalar(qtg[:, k, :], qtg[:, k, :],
                                              a_T[:, k:k + 1], None, ALU.mult)
                        if k < NKT - 1:
                            vg(e, ins)
                    return ins
                aq_evt = oneng("vector", ascale, sem=semV, waits=[qt_evt])
                for i4 in range(4):
                    il = g * 4 + i4
                    gi = h * 8 + il
                    sl = il % 2
                    psD = psX[sl]
                    psY = psX[2 + sl]
                    for s, (base, _) in enumerate(SEGS):
                        ev = mm_group(psD,
                                      [(hT[:, k, il * 128:(il + 1) * 128],
                                        wdB[:, k, :], None)
                                       for k in range(base, base + 11)],
                                      waits=[wd_seg_evt[s]],
                                      start=(s == 0), stop=(s == 3))
                        wd_seg_cons[s] = ev
                    last_bank = mm_group(psY,
                                         [(qtg[:, k, i4 * 128:(i4 + 1) * 128],
                                           woB[:, k, :], None)
                                          for k in range(NKT)],
                                         waits=[aq_evt, wo_evt])
                    x_evt = dma("sync", x_ij[sl],
                                x_in[gi * 128:(gi + 1) * 128,
                                     j * 512:(j + 1) * 512],
                                xijS[sl], waits=[xij_free[sl]])
                    bsw = [bs_loaded] if (h == 0 and j == 0 and il == 0) else []
                    vv = evac("vector", psD,
                              lambda e, psD=psD, j=j: e.tensor_tensor(
                                  ffnstg, psD,
                                  s_dB[:, j * 512:(j + 1) * 512], ALU.mult),
                              semV, waits=bsw + [gs_free[0]])
                    sS = oneng("scalar",
                               lambda e: e.activation(gstg, ffnstg, AF.Silu),
                               sem=semS, waits=[vv, cmb_free[0]])
                    gs_free[0] = sS
                    sS2 = oneng("scalar",
                                lambda e, sl=sl, gi=gi: e.activation(
                                    xnstg, x_ij[sl], AF.Identity,
                                    scale=rstdA[:, gi:gi + 1],
                                    bias=bias2A[:, gi:gi + 1]),
                                sem=semS, waits=[x_evt, cmb_free[0]])

                    def comb(e, psY=psY, sl=sl, j=j):
                        vg(e, e.tensor_tensor(s1stg, xnstg,
                                              b_outB[:, j * 512:(j + 1) * 512],
                                              ALU.add))
                        vg(e, e.tensor_tensor(s1stg, s1stg, psY, ALU.add))
                        vg(e, e.tensor_tensor(s1stg, s1stg, gstg, ALU.mult))
                        return e.tensor_tensor(ostg[sl], s1stg, x_ij[sl],
                                               ALU.add)
                    vv2 = evac("vector", psY, comb, semV, waits=[sS, sS2])
                    xij_free[sl] = vv2
                    cmb_free[0] = vv2
                    odone = dma("sync",
                                out[gi * 128:(gi + 1) * 128,
                                    j * 512:(j + 1) * 512],
                                ostg[sl], outS[sl], waits=[vv2])
                    bank_free[id(psY)] = odone
                qt_cons[0] = last_bank
            wo_cons[0] = last_bank

    for h in range(2):
        f1(h)
        f2c(h)

    out_tot = [(s, s.v) for s in outS]
    spill_tot = [(s, s.v) for s in [sxnS, sqtS[0], sqtS[1], swgS[0], swgS[1],
                                    saS] + swdS]

    def fin(e):
        for (s, v) in out_tot + spill_tot:
            if v > 0:
                e.wait_ge(s.h, v)
    plan["sync"].append(fin)

    with nc.Block() as block:
        @block.sync
        def _(e):
            for f in plan["sync"]:
                f(e)

        @block.gpsimd
        def _(e):
            for f in plan["gpsimd"]:
                f(e)

        @block.vector
        def _(e):
            for f in plan["vector"]:
                f(e)

        @block.scalar
        def _(e):
            for f in plan["scalar"]:
                f(e)

        @block.tensor
        def _(e):
            for f in plan["tensor"]:
                f(e)

    return nc


def _host_consts():
    i = np.arange(0, ROT, 2, dtype=np.float64) / ROT
    inv = 1.0 / np.power(10000.0, i)
    t = np.arange(S, dtype=np.float64)[:, None]
    fr = t * inv[None, :]
    ropc_np = np.cos(fr).astype(np.float32)
    rops_np = np.sin(fr).astype(np.float32)
    ident = np.eye(128, dtype=np.float32)
    return ropc_np, rops_np, ident


def kernel(L0_attn_X, L0_attn_W_in, L0_attn_b_in, L0_attn_W_out, L0_attn_b_out,
           L0_attn_gamma, L0_attn_beta, L0_ffn_wq_gate, L0_ffn_s_gate,
           L0_ffn_wq_up, L0_ffn_s_up, L0_ffn_wq_down, L0_ffn_s_down):
    ropc_np, rops_np, ident = _host_consts()
    W_in = np.asarray(L0_attn_W_in, dtype=np.float32)
    W_out = np.asarray(L0_attn_W_out, dtype=np.float32)
    Wg = np.asarray(L0_ffn_wq_gate, dtype=np.int32)
    Wu = np.asarray(L0_ffn_wq_up, dtype=np.int32)
    Wd = np.asarray(L0_ffn_wq_down, dtype=np.int32)
    C = lambda a: np.ascontiguousarray(a)
    shared = dict(
        wq4=C(W_in[:, :D].reshape(16, 128, 16, 128).transpose(2, 1, 0, 3)),
        wk4=C(W_in[:, D:2 * D].reshape(16, 128, 4, 512).transpose(2, 1, 0, 3)),
        wv4=C(W_in[:, 2 * D:].reshape(16, 128, 4, 512).transpose(2, 1, 0, 3)),
        wo4=C(W_out.reshape(16, 128, 4, 512).transpose(2, 1, 0, 3)),
        wg4=C(Wg.reshape(44, 128, 16, 128).transpose(0, 3, 2, 1)),
        wu4=C(Wu.reshape(44, 128, 16, 128).transpose(0, 3, 2, 1)),
        wd4=C(Wd.reshape(4, 512, 44, 128).transpose(0, 3, 2, 1)),
        b_in=C(np.asarray(L0_attn_b_in, dtype=np.float32)),
        b_out=C(np.asarray(L0_attn_b_out, dtype=np.float32)),
        gamma=C(np.asarray(L0_attn_gamma, dtype=np.float32)),
        beta=C(np.asarray(L0_attn_beta, dtype=np.float32)),
        s_g=C(np.asarray(L0_ffn_s_gate, dtype=np.float32)),
        s_u=C(np.asarray(L0_ffn_s_up, dtype=np.float32)),
        s_d=C(np.asarray(L0_ffn_s_down, dtype=np.float32)),
        ropc=ropc_np, rops=rops_np, ident=ident,
    )
    X = np.asarray(L0_attn_X, dtype=np.float32)
    nc = build_program()
    in_maps = []
    for c in range(8):
        m = dict(shared)
        m["x"] = np.ascontiguousarray(X[c])
        in_maps.append(m)
    res = run_bass_kernel_spmd(nc, in_maps, core_ids=list(range(8)))
    return np.stack([res.results[c]["out"] for c in range(8)], axis=0)


if __name__ == "__main__":
    nc = build_program()
    print("built ok, instructions:", nc.n_instructions())


# version 29
# speedup vs baseline: 1.1862x; 1.0138x over previous
import numpy as np
import ml_dtypes
import concourse.bass as bass
import concourse.mybir as mybir
from concourse.bass_utils import run_bass_kernel_spmd

F32 = mybir.dt.float32
BF16 = mybir.dt.bfloat16
I32 = mybir.dt.int32
AF = mybir.ActivationFunctionType
ALU = mybir.AluOpType

S = 2048
D = 2048
HID = 5632
NKT = D // 128
NHT = HID // 128
NT = S // 128
ROT = 1024
EPS = 1e-5
KB = 1024


class Sem:
    def __init__(self, nc, name):
        self.h = nc.alloc_semaphore(name)
        self.v = 0

    def inc(self, by=1):
        self.v += by
        return self.v


def _flat(waits):
    out = []
    for w in waits:
        if w is None:
            continue
        if isinstance(w, list):
            out.extend(x for x in w if x is not None)
        else:
            out.append(w)
    return out


def build_program():
    nc = bass.Bass("TRN2", target_bir_lowering=False)

    x_in = nc.dram_tensor("x", [S, D], F32, kind="ExternalInput").ap()
    wq4 = nc.dram_tensor("wq4", [NKT, 128, NKT, 128], F32, kind="ExternalInput").ap()
    wk4 = nc.dram_tensor("wk4", [4, 128, NKT, 512], F32, kind="ExternalInput").ap()
    wv4 = nc.dram_tensor("wv4", [4, 128, NKT, 512], F32, kind="ExternalInput").ap()
    b_in = nc.dram_tensor("b_in", [3 * D], F32, kind="ExternalInput").ap()
    wo4 = nc.dram_tensor("wo4", [4, 128, NKT, 512], F32, kind="ExternalInput").ap()
    b_out = nc.dram_tensor("b_out", [D], F32, kind="ExternalInput").ap()
    gamma = nc.dram_tensor("gamma", [D], F32, kind="ExternalInput").ap()
    beta = nc.dram_tensor("beta", [D], F32, kind="ExternalInput").ap()
    wg4 = nc.dram_tensor("wg4", [NHT, 128, NKT, 128], I32, kind="ExternalInput").ap()
    wu4 = nc.dram_tensor("wu4", [NHT, 128, NKT, 128], I32, kind="ExternalInput").ap()
    wd4 = nc.dram_tensor("wd4", [4, 128, NHT, 512], I32, kind="ExternalInput").ap()
    s_g = nc.dram_tensor("s_g", [HID], F32, kind="ExternalInput").ap()
    s_u = nc.dram_tensor("s_u", [HID], F32, kind="ExternalInput").ap()
    s_d = nc.dram_tensor("s_d", [D], F32, kind="ExternalInput").ap()
    ropc = nc.dram_tensor("ropc", [S, ROT // 2], F32, kind="ExternalInput").ap()
    rops = nc.dram_tensor("rops", [S, ROT // 2], F32, kind="ExternalInput").ap()
    ident_in = nc.dram_tensor("ident", [128, 128], F32, kind="ExternalInput").ap()

    out = nc.dram_tensor("out", [S, D], F32, kind="ExternalOutput").ap()

    qts = nc.dram_tensor("qts", [NKT, 128, S], BF16, kind="Internal").ap()
    xnts = nc.dram_tensor("xnts", [NKT, 128, S], BF16, kind="Internal").ap()
    wgs4 = nc.dram_tensor("wgs4", [NHT, 128, NKT, 128], BF16, kind="Internal").ap()
    wus4 = nc.dram_tensor("wus4", [NHT, 128, NKT, 128], BF16, kind="Internal").ap()
    wds4 = nc.dram_tensor("wds4", [4, 128, NHT, 512], BF16, kind="Internal").ap()
    a_scr = nc.dram_tensor("a_scr", [1, D], F32, kind="Internal").ap()

    def sb_at(name, shape, dtype, at):
        h = nc.alloc_sbuf_tensor_at(name, shape, dtype, offset=at)
        per_part = int(np.prod(shape[1:])) * mybir.dt.size(dtype)
        return h.ap(), at + per_part

    class Region:
        def __init__(self, start):
            self.ptr = start

        def al(self, name, shape, dtype):
            at = (self.ptr + 31) // 32 * 32
            ap, end = sb_at(name, shape, dtype, at)
            self.ptr = end
            return ap

    SBUF_BASE = 16512
    LIMIT = 229344
    G = Region(SBUF_BASE)
    idtf = G.al("idtf", [128, 128], F32)
    idtb = G.al("idtb", [128, 128], BF16)
    binP = G.al("binP", [128, 48], F32)
    sgP = G.al("sgP", [128, NHT], F32)
    suP = G.al("suP", [128, NHT], F32)
    gamT = G.al("gamT", [128, NKT], F32)
    betT = G.al("betT", [128, NKT], F32)
    mnegA = G.al("mnegA", [128, NT], F32)
    rstdA = G.al("rstdA", [128, NT], F32)
    bias2A = G.al("bias2A", [128, NT], F32)
    r2A = G.al("r2A", [128, NT], F32)
    ssqKA = G.al("ssqKA", [128, 8], F32)
    ssqVA = G.al("ssqVA", [128, 8], F32)
    wvec = G.al("wvec", [128, 8], BF16)
    tmpA = G.al("tmpA", [128, 8], F32)
    tmp1A = G.al("tmp1A", [128, 4], F32)
    epsP = G.al("epsP", [128, 1], F32)
    a_T = G.al("a_T", [128, NKT], F32)
    bnst = [G.al("bnst0", [128, 24], F32), G.al("bnst1", [128, 24], F32)]
    mvst = [G.al("mvst0", [128, 2], F32), G.al("mvst1", [128, 2], F32)]
    GLOBAL_END = G.ptr

    P = Region(GLOBAL_END)
    xn2T = P.al("xn2T", [128, NKT, KB], BF16)
    pbuf = P.al("pbuf", [128, 8, D], BF16)
    binBK = P.al("binBK", [128, D], F32)
    binBV = P.al("binBV", [128, D], F32)
    _xoff = (P.ptr + 31) // 32 * 32
    x_t = [P.al("x_t0", [128, D], F32), P.al("x_t1", [128, D], F32)]
    _xnoff = (P.ptr + 31) // 32 * 32
    xn_t = [P.al("xn_t0", [128, D], F32), P.al("xn_t1", [128, D], F32)]
    xnr2b = [P.al(f"xnr2b{i}", [128, D], BF16) for i in range(4)]
    kvbK2, _ = sb_at("kvbK2", [128, NKT, 512], BF16, _xoff)
    kvbV2, _ = sb_at("kvbV2", [128, NKT, 512], BF16, _xnoff)
    xntg = P.al("xntg", [128, NKT, 256], BF16)
    a_row = P.al("a_row", [1, D], F32)
    kvbK = P.al("kvbK", [128, NKT, 512], BF16)
    kvbV = P.al("kvbV", [128, NKT, 512], BF16)
    wqB = [P.al("wqB0", [128, NKT, 128], BF16), P.al("wqB1", [128, NKT, 128], BF16)]
    qtstg = [P.al("qtstg0", [128, 512], BF16), P.al("qtstg1", [128, 512], BF16)]
    krst = [P.al("krst0", [128, 512], F32), P.al("krst1", [128, 512], F32)]
    vrst = [P.al("vrst0", [128, 512], F32), P.al("vrst1", [128, 512], F32)]
    sqscr = P.al("sqscr", [128, 512], F32)
    vscr = P.al("vscr", [128, 512], F32)
    ropcB = [P.al("ropcB0", [128, 256], F32), P.al("ropcB1", [128, 256], F32)]
    ropsB = [P.al("ropsB0", [128, 256], F32), P.al("ropsB1", [128, 256], F32)]
    assert P.ptr <= LIMIT, f"P1 over SBUF: {P.ptr}"

    F = Region(GLOBAL_END)
    hT = F.al("hT", [128, NHT, KB], BF16)
    HT_END = F.ptr
    xnT = F.al("xnT", [128, NKT, KB], BF16)
    wgB = [F.al("wgB0", [128, NKT, 128], BF16), F.al("wgB1", [128, NKT, 128], BF16)]
    wuB = [F.al("wuB0", [128, NKT, 128], BF16), F.al("wuB1", [128, NKT, 128], BF16)]
    sigstg = F.al("sigstg", [128, 512], F32)
    upstg = F.al("upstg", [128, 512], F32)
    assert F.ptr <= LIMIT, f"F1 over SBUF: {F.ptr}"

    F2 = Region(HT_END)
    wdB = F2.al("wdB", [128, NHT, 512], BF16)
    woB = F2.al("woB", [128, NKT, 512], BF16)
    qtg = F2.al("qtg", [128, NKT, 512], BF16)
    b_outB = F2.al("b_outB", [128, D], F32)
    s_dB = F2.al("s_dB", [128, D], F32)
    x_ij = [F2.al("x_ij0", [128, 512], F32), F2.al("x_ij1", [128, 512], F32)]
    ffnstg = F2.al("ffnstg", [128, 512], F32)
    gstg = F2.al("gstg", [128, 512], F32)
    xnstg = F2.al("xnstg", [128, 512], F32)
    s1stg = F2.al("s1stg", [128, 512], F32)
    ostg = [F2.al("ostg0", [128, 512], F32), F2.al("ostg1", [128, 512], F32)]
    assert F2.ptr <= LIMIT, f"F2C over SBUF: {F2.ptr}"

    psX = [nc.alloc_psum_tensor(f"psX{i}", [128, 512], F32).ap() for i in range(4)]
    psT = [nc.alloc_psum_tensor(f"psT{i}", [128, 512], F32).ap() for i in range(2)]
    psA = nc.alloc_psum_tensor("psA", [128, 512], F32).ap()

    dmaI = Sem(nc, "dmaI")
    xldS = [Sem(nc, "xld0"), Sem(nc, "xld1")]
    ropeS = [Sem(nc, "rope0"), Sem(nc, "rope1")]
    sxnS = Sem(nc, "sxn")
    wqS = [Sem(nc, "wq0"), Sem(nc, "wq1")]
    sqtS = [Sem(nc, "sqt0"), Sem(nc, "sqt1")]
    kvS = [Sem(nc, "kv0"), Sem(nc, "kv1")]
    saS = Sem(nc, "saS")
    aTS = Sem(nc, "aTS")
    bsS = Sem(nc, "bsS")
    xnlS = Sem(nc, "xnlS")
    wguS = [Sem(nc, "wgu0"), Sem(nc, "wgu1")]
    wglS = [Sem(nc, "wgl0"), Sem(nc, "wgl1")]
    swgS = [Sem(nc, "swg0"), Sem(nc, "swg1")]
    wdlS = [Sem(nc, f"wdl{s}") for s in range(4)]
    wdl2S = [Sem(nc, f"wdm{s}") for s in range(4)]
    swdS = [Sem(nc, f"swd{s}") for s in range(4)]
    woS = Sem(nc, "woS")
    qtlS = Sem(nc, "qtlS")
    xijS = [Sem(nc, "xij0"), Sem(nc, "xij1")]
    ibS = Sem(nc, "ibS")
    outS = [Sem(nc, "out0"), Sem(nc, "out1")]
    semT = Sem(nc, "semT")
    semV = Sem(nc, "semV")
    semS = Sem(nc, "semS")
    semG = Sem(nc, "semG")
    semW = Sem(nc, "semW")
    wctr = [0]

    def vg(e, ins):
        wctr[0] += 1
        ins.then_inc(semW.h, 1)
        e.wait_ge(semW.h, wctr[0])

    semY = Sem(nc, "semY")
    yctr = [0]

    def sg(e, ins):
        yctr[0] += 1
        ins.then_inc(semY.h, 1)
        e.wait_ge(semY.h, yctr[0])

    plan = {"sync": [], "gpsimd": [], "vector": [], "scalar": [], "tensor": []}

    def dma(q, out_ap, in_ap, sem, waits=(), slow=False):
        wl = _flat(waits)

        def f(e, wl=wl, out_ap=out_ap, in_ap=in_ap, sem=sem, slow=slow):
            for (s, v) in wl:
                if v > 0:
                    e.wait_ge(s.h, v)
            e.dma_start(out_ap, in_ap,
                        allow_slow_non_contiguous=slow).then_inc(sem.h, 16)

        plan[q].append(f)
        sem.inc(16)
        return (sem, sem.v)

    eng_chain = {"vector": semV, "scalar": semS, "gpsimd": semG}

    def oneng(eng, fn, sem=None, waits=()):
        ch = eng_chain.get(eng)
        if ch is not None:
            assert sem is None or sem is ch
            sem = ch
        wl = _flat(list(waits))
        byid = {}
        for (s, v) in wl:
            if v > byid.get(id(s), (None, 0))[1]:
                byid[id(s)] = (s, v)
        wl = list(byid.values())
        if sem is not None:
            sem.inc(1)

        def f(e, wl=wl, fn=fn, sem=sem):
            for (s, v) in wl:
                if v > 0:
                    e.wait_ge(s.h, v)
            ins = fn(e)
            if sem is not None:
                assert ins is not None, "producer fn must return its last instr"
                ins.then_inc(sem.h, 1)

        plan[eng].append(f)
        return (sem, sem.v) if sem is not None else None

    bank_free = {}
    bank_ready = {}

    def mm_group(ps, mms, waits=(), start=True, stop=True):
        wl = _flat(list(waits) + ([bank_free.get(id(ps))] if start else []))

        def f(e, wl=wl, mms=mms, start=start, stop=stop):
            for (s, v) in wl:
                if v > 0:
                    e.wait_ge(s.h, v)
            n = len(mms)
            ins = None
            for i, (l, r, o) in enumerate(mms):
                ins = e.matmul(o if o is not None else ps, l, r,
                               start=(start and i == 0),
                               stop=(stop and i == n - 1))
            ins.then_inc(semT.h, 1)

        plan["tensor"].append(f)
        semT.inc(1)
        evt = (semT, semT.v)
        bank_ready[id(ps)] = evt
        return evt

    def tp_group(ps, items, waits=()):
        wl = _flat(list(waits) + [bank_free.get(id(ps))])

        def f(e, wl=wl, items=items):
            for (s, v) in wl:
                if v > 0:
                    e.wait_ge(s.h, v)
            ins = None
            for (src, o) in items:
                ins = e.transpose(o, src, idtf)
            ins.then_inc(semT.h, 1)

        plan["tensor"].append(f)
        semT.inc(1)
        evt = (semT, semT.v)
        bank_ready[id(ps)] = evt
        return evt

    def evac(eng_name, ps, fn, sem, waits=()):
        evt = oneng(eng_name, fn, sem=sem,
                    waits=list(waits) + [bank_ready[id(ps)]])
        bank_free[id(ps)] = evt
        return evt

    dma("sync", idtf, ident_in, dmaI)
    idtb_done = dma("gpsimd", idtb, ident_in, ibS)
    dma("sync", binP, b_in.rearrange("(m p) -> p m", p=128), dmaI, slow=True)
    dma("sync", sgP, s_g.rearrange("(m p) -> p m", p=128), dmaI, slow=True)
    dma("sync", suP, s_u.rearrange("(m p) -> p m", p=128), dmaI, slow=True)
    dma("sync", gamT, gamma.rearrange("(k p) -> p k", p=128), dmaI, slow=True)
    dma("sync", betT, beta.rearrange("(k p) -> p k", p=128), dmaI, slow=True)
    dma("sync", binBK, b_in[D:2 * D].rearrange("(o d) -> o d", o=1).broadcast_to([128, D]), dmaI)
    init_done = dma("sync", binBV,
                    b_in[2 * D:3 * D].rearrange("(o d) -> o d", o=1).broadcast_to([128, D]),
                    dmaI)
    a_zero_evt = oneng("vector", lambda e: e.memset(a_row, 0.0), sem=semV,
                       waits=[init_done])
    eps_evt = oneng("vector", lambda e: e.memset(epsP, EPS), sem=semV)
    def load_set6(e):
        return e.add_instruction(mybir.InstLoadActFuncSet(
            name=nc.get_next_instruction_name(), act_func_set_id=6,
            ins=[], outs=[]))
    oneng("scalar", load_set6, waits=[init_done])

    x_cons = [None, None]
    xn_cons = [None, None]
    xr2b_evt = [None] * 4
    xr2b_cons = [None]
    bnst_free = [None, None]
    xntg_free = [None]
    pend_spill = [None]
    xn2T_guard = [None]
    kv2_guard = [None]

    def p1_stepA_tile(h, il):
        gi = h * 8 + il
        par = gi % 2
        sl = il % 2
        quad = il % 4
        tb = par * 4
        lnc = tmpA[:, tb + 0:tb + 1]
        t2 = tmpA[:, tb + 1:tb + 2]

        alias_g = [kv2_guard[0]] if (h == 1 and il < 2) else []
        x_evt = dma("sync", x_t[par], x_in[gi * 128:(gi + 1) * 128, :],
                    xldS[par], waits=[x_cons[par]] + alias_g)
        if pend_spill[0] is not None:
            g2d, evd = pend_spill[0]
            pend_spill[0] = None
            last = None
            for part in range(4):
                last = dma("sync",
                           xnts[part * 4:(part + 1) * 4, :,
                                g2d * 256:(g2d + 1) * 256]
                           .rearrange("k p t -> p k t"),
                           xntg[:, part * 4:(part + 1) * 4, :],
                           sxnS, waits=evd if part == 0 else [])
            xntg_free[0] = last

        def bnf(e, par=par):
            ins = None
            for c in range(4):
                ins = e.bn_stats(bnst[par][:, c * 6:(c + 1) * 6],
                                 x_t[par][:, c * 512:(c + 1) * 512])
                if c < 3:
                    vg(e, ins)
            return ins
        bn_done = oneng("vector", bnf, sem=semV,
                        waits=[x_evt, bnst_free[par]])
        agg_done = oneng("vector",
                         lambda e, par=par: e.bn_aggr(mvst[par], bnst[par]),
                         sem=semV, waits=[bn_done])
        bnst_free[par] = agg_done

        ev1 = oneng("vector",
                    lambda e, par=par, gi=gi: e.tensor_scalar(
                        mnegA[:, gi:gi + 1], mvst[par][:, 0:1],
                        -1.0, None, ALU.mult),
                    sem=semV, waits=[agg_done])

        def r1f(e, par=par, gi=gi, lnc=lnc):
            sg(e, e.activation(lnc, mvst[par][:, 1:2], AF.Ln, bias=epsP))
            return e.activation(rstdA[:, gi:gi + 1], lnc, AF.Exp, scale=-0.5)
        sc1 = oneng("scalar", r1f, sem=semS, waits=[agg_done, eps_evt])

        def fn2(e, par=par, gi=gi, t2=t2):
            vg(e, e.tensor_tensor(bias2A[:, gi:gi + 1], mnegA[:, gi:gi + 1],
                                  rstdA[:, gi:gi + 1], ALU.mult))
            vg(e, e.tensor_tensor(t2, mvst[par][:, 1:2],
                                  rstdA[:, gi:gi + 1], ALU.mult))
            return e.tensor_tensor(t2, t2, rstdA[:, gi:gi + 1], ALU.mult)
        ev2 = oneng("vector", fn2, sem=semV, waits=[sc1, ev1])

        def r2f(e, gi=gi, lnc=lnc, t2=t2):
            sg(e, e.activation(lnc, t2, AF.Ln, bias=epsP))
            return e.activation(r2A[:, gi:gi + 1], lnc, AF.Exp, scale=-0.5)
        sc2 = oneng("scalar", r2f, sem=semS, waits=[ev2])

        xn_evt = oneng("gpsimd",
                       lambda e, par=par, gi=gi: e.tensor_scalar(
                           xn_t[par], x_t[par], mnegA[:, gi:gi + 1],
                           rstdA[:, gi:gi + 1], ALU.add, ALU.mult),
                       sem=semG, waits=[x_evt, ev1, sc1, xn_cons[par]] + alias_g)
        x_cons[par] = [bn_done, xn_evt]

        xr2_evt = oneng("scalar",
                        lambda e, par=par, gi=gi, quad=quad: e.activation(
                            xnr2b[quad], xn_t[par], AF.Identity,
                            scale=r2A[:, gi:gi + 1]),
                        sem=semS, waits=[xn_evt, sc2, xr2b_cons[0]])
        xr2b_evt[quad] = xr2_evt

        last_tp = None
        for g in range(4):
            ps = psT[g % 2]
            items = [(xn_t[par][:, (4 * g + q) * 128:(4 * g + q + 1) * 128],
                      ps[:, q * 128:(q + 1) * 128]) for q in range(4)]
            last_tp = tp_group(ps, items, waits=[xn_evt, init_done])
            ew = [xntg_free[0]] if (sl == 0 and g == 0) else []

            def ev(e, ps=ps, g=g, sl=sl):
                return e.tensor_copy(
                    xntg[:, 4 * g:4 * (g + 1), sl * 128:(sl + 1) * 128],
                    ps.rearrange("p (q t) -> p q t", q=4))
            evac("vector", ps, ev, semV, waits=ew)
        xn_cons[par] = [xr2_evt, last_tp]

        if sl == 1:
            pend_spill[0] = (gi // 2, [(semV, semV.v), (semS, semS.v)])

        if quad == 3:
            a4 = il // 4
            last_tp2 = None
            for kt in range(NKT):
                ps = psX[kt % 4]
                mms = [(xnr2b[q][:, kt * 128:(kt + 1) * 128], idtb,
                        ps[:, q * 128:(q + 1) * 128]) for q in range(4)]
                w0 = list(xr2b_evt) + [idtb_done]
                last_tp2 = mm_group(ps, mms, waits=w0)
                gw = [xn2T_guard[0]] if (h == 1 and a4 == 0 and kt < 4) else []

                def fxs(e, ps=ps, kt=kt, a4=a4):
                    return e.activation(
                        xn2T[:, kt, a4 * 512:(a4 + 1) * 512], ps, AF.Identity,
                        scale=gamT[:, kt:kt + 1], bias=betT[:, kt:kt + 1])
                evac("scalar", ps, fxs, semS, waits=gw)
            xr2b_cons[0] = last_tp2

    wq_cons = [None, None]
    stg_free = [None, None]

    def p1_stepB(h, a_done):
        for m in range(NKT):
            sl = m % 2
            w_evt = dma("gpsimd", wqB[sl], wq4[m],
                        wqS[sl],
                        waits=[wq_cons[sl]] + ([init_done] if m == 0 else []))
            last_grp = None
            for n in range(2):
                ps = psX[(2 * m + n) % 4]
                mms = [(wqB[sl][:, k, :], xn2T[:, k, n * 512:(n + 1) * 512], None)
                       for k in range(NKT)]
                extra = a_done if (m == 0 and n == 0) else []
                last_grp = mm_group(ps, mms, waits=[w_evt] + extra)
                st = qtstg[n]

                def ev(e, ps=ps, st=st, m=m):
                    return e.activation(st, ps, AF.Identity,
                                        bias=binP[:, m:m + 1])
                vv = evac("scalar", ps, ev, semS, waits=[stg_free[n]])
                stg_free[n] = dma("sync",
                                  qts[m, :, h * KB + n * 512:h * KB + (n + 1) * 512],
                                  st, sqtS[n], waits=[vv])
            wq_cons[sl] = last_grp

    kv_cons = [None, None]
    rope_cons = [None, None]

    def p1_stepC(h, a_eng_done):
        k_free = [None, None]
        v_free = [None, None]
        pbuf_guard = (semT, semT.v)
        z1 = oneng("vector", lambda e: e.memset(ssqKA, 0.0), sem=semV,
                   waits=[pbuf_guard, init_done])
        z2 = oneng("vector", lambda e: e.memset(ssqVA, 0.0), sem=semV)
        zevt = z2
        for c in range(4):
            slot = c % 2
            bufK = kvbK if slot == 0 else kvbK2
            bufV = kvbV if slot == 0 else kvbV2
            w0 = [kv_cons[slot]]
            if c == 1:
                w0 = w0 + a_eng_done
            dma("gpsimd", bufK, wk4[c], kvS[slot], waits=w0)
            kv_evt = dma("gpsimd", bufV, wv4[c], kvS[slot], waits=[])
            for il in range(8):
                gi = h * 8 + il
                rsl = il % 2
                kb = krst[rsl]
                vb = vrst[rsl]
                rope_evt = None
                if c < 2:
                    dma("sync", ropcB[rsl],
                        ropc[gi * 128:(gi + 1) * 128, c * 256:(c + 1) * 256],
                        ropeS[rsl], waits=[rope_cons[rsl]])
                    rope_evt = dma("sync", ropsB[rsl],
                                   rops[gi * 128:(gi + 1) * 128,
                                        c * 256:(c + 1) * 256],
                                   ropeS[rsl], waits=[])
                psK = psX[(2 * il) % 4]
                psV = psX[(2 * il + 1) % 4]
                mm_group(psK, [(xn2T[:, k, il * 128:(il + 1) * 128],
                                bufK[:, k, :], None) for k in range(NKT)],
                         waits=[kv_evt])
                mm_group(psV, [(xn2T[:, k, il * 128:(il + 1) * 128],
                                bufV[:, k, :], None) for k in range(NKT)],
                         waits=[kv_evt])
                if il == 7:
                    kv_cons[slot] = bank_ready[id(psV)]

                def kpath(e, psK=psK, c=c, rsl=rsl, kb=kb):
                    ins = e.tensor_tensor(kb, psK,
                                          binBK[:, c * 512:(c + 1) * 512], ALU.add)
                    if c < 2:
                        vg(e, ins)
                        cs, sn = ropcB[rsl], ropsB[rsl]
                        ke = kb[:, 0:512:2]
                        ko = kb[:, 1:512:2]
                        vg(e, e.tensor_tensor(vscr[:, 0:256], ke, sn, ALU.mult))
                        vg(e, e.tensor_tensor(vscr[:, 256:512], ko, sn, ALU.mult))
                        vg(e, e.tensor_tensor(ke, ke, cs, ALU.mult))
                        vg(e, e.tensor_tensor(ko, ko, cs, ALU.mult))
                        vg(e, e.tensor_tensor(ke, ke, vscr[:, 256:512], ALU.subtract))
                        ins = e.tensor_tensor(ko, ko, vscr[:, 0:256], ALU.add)
                    return ins
                kw = [bank_ready[id(psK)], zevt, k_free[rsl]]
                if c < 2:
                    kw.append(rope_evt)
                vvK = oneng("vector", kpath, sem=semV, waits=kw)
                bank_free[id(psK)] = vvK
                sS = oneng("scalar",
                           lambda e, kb=kb: e.activation(
                               sqscr, kb, AF.Square, accum_out=tmp1A[:, 0:1]),
                           sem=semS, waits=[vvK])
                k_free[rsl] = sS
                vv2 = oneng("vector",
                            lambda e, il=il: e.tensor_tensor(
                                ssqKA[:, il:il + 1], ssqKA[:, il:il + 1],
                                tmp1A[:, 0:1], ALU.add),
                            sem=semV, waits=[sS])

                def vpath(e, psV=psV, c=c, rsl=rsl, vb=vb):
                    ins = e.tensor_tensor(vb, psV,
                                          binBV[:, c * 512:(c + 1) * 512], ALU.add)
                    if c < 2:
                        vg(e, ins)
                        cs, sn = ropcB[rsl], ropsB[rsl]
                        ve_ = vb[:, 0:512:2]
                        vo = vb[:, 1:512:2]
                        vg(e, e.tensor_tensor(vscr[:, 0:256], ve_, sn, ALU.mult))
                        vg(e, e.tensor_tensor(vscr[:, 256:512], vo, sn, ALU.mult))
                        vg(e, e.tensor_tensor(ve_, ve_, cs, ALU.mult))
                        vg(e, e.tensor_tensor(vo, vo, cs, ALU.mult))
                        vg(e, e.tensor_tensor(ve_, ve_, vscr[:, 256:512], ALU.subtract))
                        ins = e.tensor_tensor(vo, vo, vscr[:, 0:256], ALU.add)
                    return ins
                vw = [bank_ready[id(psV)], v_free[rsl]]
                if c < 2:
                    vw.append(rope_evt)
                vvV = oneng("vector", vpath, sem=semV, waits=vw)
                bank_free[id(psV)] = vvV
                if c < 2:
                    rope_cons[rsl] = vvV
                sS2 = oneng("scalar",
                            lambda e, vb=vb: e.activation(
                                sqscr, vb, AF.Square, accum_out=tmp1A[:, 1:2]),
                            sem=semS, waits=[vvV])
                vv4 = oneng("vector",
                            lambda e, il=il: e.tensor_tensor(
                                ssqVA[:, il:il + 1], ssqVA[:, il:il + 1],
                                tmp1A[:, 1:2], ALU.add),
                            sem=semV, waits=[sS2])
                pv = oneng("vector",
                           lambda e, il=il, c=c, kb=kb, vb=vb: e.tensor_tensor(
                               pbuf[:, il, c * 512:(c + 1) * 512], kb, vb,
                               ALU.mult),
                           sem=semV, waits=[vv2, vv4])
                v_free[rsl] = [sS2, pv]

        wlast = [None, None]
        for il in range(8):
            p2 = il % 2
            vv = oneng("vector",
                       lambda e, il=il, p2=p2: e.tensor_tensor(
                           tmp1A[:, p2:p2 + 1], ssqKA[:, il:il + 1],
                           ssqVA[:, il:il + 1], ALU.mult),
                       sem=semV, waits=[wlast[p2]])

            def wf(e, il=il, p2=p2):
                sg(e, e.activation(tmp1A[:, 2 + p2:3 + p2],
                                   tmp1A[:, p2:p2 + 1], AF.Ln))
                return e.activation(wvec[:, il:il + 1],
                                    tmp1A[:, 2 + p2:3 + p2], AF.Exp,
                                    scale=-0.5)
            wlast[p2] = oneng("scalar", wf, sem=semS, waits=[vv])
        w_ready = [(semS, semS.v)]
        for c2 in range(4):
            mms = [(wvec[:, il:il + 1], pbuf[:, il, c2 * 512:(c2 + 1) * 512],
                    psA[0:1, :]) for il in range(8)]
            mm_group(psA, mms, waits=w_ready + [a_zero_evt])

            def aev(e, c2=c2):
                return e.tensor_tensor(a_row[0:1, c2 * 512:(c2 + 1) * 512],
                                       a_row[0:1, c2 * 512:(c2 + 1) * 512],
                                       psA[0:1, :], ALU.add)
            evac("vector", psA, aev, semV)

    for h in range(2):
        for il in range(8):
            p1_stepA_tile(h, il)
        if pend_spill[0] is not None:
            g2d, evd = pend_spill[0]
            pend_spill[0] = None
            last = None
            for part in range(4):
                last = dma("sync",
                           xnts[part * 4:(part + 1) * 4, :,
                                g2d * 256:(g2d + 1) * 256]
                           .rearrange("k p t -> p k t"),
                           xntg[:, part * 4:(part + 1) * 4, :],
                           sxnS, waits=evd if part == 0 else [])
            xntg_free[0] = last
        a_done = [(semS, semS.v), (semV, semV.v)]
        a_eng_done = [(semS, semS.v), (semV, semV.v), (semT, semT.v),
                      (semG, semG.v)]
        p1_stepB(h, a_done)
        p1_stepC(h, a_eng_done)
        kv2_guard[0] = kv_cons[1]
        xn2T_guard[0] = (semT, semT.v)

    vv = oneng("vector",
               lambda e: e.tensor_scalar(a_row, a_row, 0.5, 0.5,
                                         ALU.mult, ALU.add),
               sem=semV)
    ase = dma("sync", a_scr, a_row, saS, waits=[vv])
    aT_ready = dma("sync", a_T, a_scr.rearrange("o (k p) -> p (o k)", p=128),
                   aTS, waits=[ase], slow=True)

    p1_all_done = [(semV, semV.v), (semS, semS.v), (semT, semT.v),
                   (semG, semG.v), (sxnS, sxnS.v),
                   (sqtS[0], sqtS[0].v), (sqtS[1], sqtS[1].v)]
    dma("sync", b_outB,
        b_out.rearrange("(o d) -> o d", o=1).broadcast_to([128, D]), bsS,
        waits=p1_all_done)
    bs_loaded = dma("sync", s_dB,
                    s_d.rearrange("(o d) -> o d", o=1).broadcast_to([128, D]),
                    bsS)

    wf_cons = [None, None]
    wf_spill = [None, None]
    hs_free = [None]

    def f1(h):
        if h == 0:
            oneng("scalar", lambda e: e.preload_act_table(AF.Silu))
        f2c_guard = ([(semT, semT.v)] +
                     [(s, s.v) for s in swdS]) if h == 1 else []
        xnT_evt = None
        for part in range(4):
            w0 = []
            if part == 0:
                w0 = p1_all_done if h == 0 else f2c_guard
            xnT_evt = dma("sync", xnT[:, part * 4:(part + 1) * 4, :],
                          xnts[part * 4:(part + 1) * 4, :, h * KB:(h + 1) * KB]
                          .rearrange("k p t -> p k t"),
                          xnlS, waits=w0)
        for m in range(NHT):
            sl = m % 2
            reuse = [wf_cons[sl], wf_spill[sl]]
            if h == 0:
                dma("gpsimd", wgB[sl], wg4[m], wguS[sl],
                    waits=[xnT_evt] + reuse)
                w_evt = dma("gpsimd", wuB[sl], wu4[m], wguS[sl], waits=[])

                def sub128(e, sl=sl):
                    vg(e, e.tensor_scalar(wgB[sl].rearrange("p k c -> p (k c)"),
                                          wgB[sl].rearrange("p k c -> p (k c)"),
                                          -128.0, None, ALU.add))
                    return e.tensor_scalar(wuB[sl].rearrange("p k c -> p (k c)"),
                                           wuB[sl].rearrange("p k c -> p (k c)"),
                                           -128.0, None, ALU.add)
                w_evt = oneng("vector", sub128, sem=semV, waits=[w_evt])
                dma("sync", wgs4[m], wgB[sl], swgS[sl], waits=[w_evt])
                wf_spill[sl] = dma("sync", wus4[m], wuB[sl], swgS[sl],
                                   waits=[])
            else:
                dma("sync", wgB[sl], wgs4[m], wglS[sl],
                    waits=([xnT_evt] + reuse))
                w_evt = dma("sync", wuB[sl], wus4[m], wglS[sl], waits=[])
                wf_spill[sl] = None
            lastg = None
            for n in range(2):
                psG = psX[(2 * m + n) % 2]
                psU = psX[2 + (2 * m + n) % 2]
                mm_group(psG, [(wgB[sl][:, k, :],
                                xnT[:, k, n * 512:(n + 1) * 512], None)
                               for k in range(NKT)], waits=[w_evt])
                lastg = mm_group(psU, [(wuB[sl][:, k, :],
                                        xnT[:, k, n * 512:(n + 1) * 512], None)
                                       for k in range(NKT)], waits=[w_evt])
                sSe = evac("scalar", psG,
                           lambda e, psG=psG, m=m: e.activation(
                               sigstg, psG, AF.Silu, scale=sgP[:, m:m + 1]),
                           semS, waits=[hs_free[0]])
                vVe = evac("vector", psU,
                           lambda e, psU=psU, m=m: e.tensor_scalar(
                               upstg, psU, suP[:, m:m + 1], None, ALU.mult),
                           semV, waits=[hs_free[0]])
                vv2 = oneng("vector",
                            lambda e, m=m, n=n: e.tensor_tensor(
                                hT[:, m, n * 512:(n + 1) * 512], sigstg, upstg,
                                ALU.mult),
                            sem=semV, waits=[sSe, vVe])
                hs_free[0] = vv2
                bank_free[id(psG)] = vv2
                bank_free[id(psU)] = vv2
            wf_cons[sl] = lastg

    wd_seg_cons = [None] * 4
    wd_seg_spill = [None] * 4
    wo_cons = [None]
    qt_cons = [None]
    xij_free = [None, None]
    gs_free = [None]
    cmb_free = [None]
    SEGS = [(11 * s, [(11 * s, 4), (11 * s + 4, 4), (11 * s + 8, 3)])
            for s in range(4)]

    def f2c(h):
        f1_done = [(semV, semV.v), (semT, semT.v)]
        if h == 0:
            f1_done += [(swgS[0], swgS[0].v), (swgS[1], swgS[1].v)]
        for j in range(4):
            wd_seg_evt = [None] * 4
            for s, (base, parts) in enumerate(SEGS):
                w0 = [wd_seg_cons[s], wd_seg_spill[s]]
                if j == 0:
                    w0 = w0 + f1_done
                if h == 0:
                    ld = dma("gpsimd", wdB[:, base:base + 11, :],
                             wd4[j, :, base:base + 11, :],
                             wdlS[s], waits=w0)
                    sub = oneng("vector",
                                lambda e, base=base: e.tensor_scalar(
                                    wdB[:, base:base + 11, :]
                                    .rearrange("p k c -> p (k c)"),
                                    wdB[:, base:base + 11, :]
                                    .rearrange("p k c -> p (k c)"),
                                    -128.0, None, ALU.add),
                                sem=semV, waits=[ld])
                    wd_seg_spill[s] = dma("sync",
                                          wds4[j, :, base:base + 11, :],
                                          wdB[:, base:base + 11, :],
                                          swdS[s], waits=[sub])
                    wd_seg_evt[s] = sub
                else:
                    ld = dma("sync", wdB[:, base:base + 11, :],
                             wds4[j, :, base:base + 11, :],
                             wdl2S[s], waits=w0)
                    wd_seg_spill[s] = None
                    wd_seg_evt[s] = ld
            wo_evt = dma("gpsimd", woB, wo4[j], woS,
                         waits=f1_done + [wo_cons[0]])

            last_bank = None
            for g in range(2):
                qt_evt = None
                for part in range(4):
                    qt_evt = dma("sync", qtg[:, part * 4:(part + 1) * 4, :],
                                 qts[part * 4:(part + 1) * 4, :,
                                     h * KB + g * 512:h * KB + (g + 1) * 512]
                                 .rearrange("k p t -> p k t"),
                                 qtlS,
                                 waits=([aT_ready, qt_cons[0]] + f1_done)
                                 if part == 0 else [])

                def ascale(e):
                    ins = None
                    for k in range(NKT):
                        ins = e.tensor_scalar(qtg[:, k, :], qtg[:, k, :],
                                              a_T[:, k:k + 1], None, ALU.mult)
                        if k < NKT - 1:
                            vg(e, ins)
                    return ins
                aq_evt = oneng("vector", ascale, sem=semV, waits=[qt_evt])
                for i4 in range(4):
                    il = g * 4 + i4
                    gi = h * 8 + il
                    sl = il % 2
                    psD = psX[sl]
                    psY = psX[2 + sl]
                    for s, (base, _) in enumerate(SEGS):
                        ev = mm_group(psD,
                                      [(hT[:, k, il * 128:(il + 1) * 128],
                                        wdB[:, k, :], None)
                                       for k in range(base, base + 11)],
                                      waits=[wd_seg_evt[s]],
                                      start=(s == 0), stop=(s == 3))
                        wd_seg_cons[s] = ev
                    last_bank = mm_group(psY,
                                         [(qtg[:, k, i4 * 128:(i4 + 1) * 128],
                                           woB[:, k, :], None)
                                          for k in range(NKT)],
                                         waits=[aq_evt, wo_evt])
                    x_evt = dma("sync", x_ij[sl],
                                x_in[gi * 128:(gi + 1) * 128,
                                     j * 512:(j + 1) * 512],
                                xijS[sl], waits=[xij_free[sl]])
                    bsw = [bs_loaded] if (h == 0 and j == 0 and il == 0) else []
                    vv = evac("vector", psD,
                              lambda e, psD=psD, j=j: e.tensor_tensor(
                                  ffnstg, psD,
                                  s_dB[:, j * 512:(j + 1) * 512], ALU.mult),
                              semV, waits=bsw + [gs_free[0]])
                    sS = oneng("scalar",
                               lambda e: e.activation(gstg, ffnstg, AF.Silu),
                               sem=semS, waits=[vv, cmb_free[0]])
                    gs_free[0] = sS
                    sS2 = oneng("scalar",
                                lambda e, sl=sl, gi=gi: e.activation(
                                    xnstg, x_ij[sl], AF.Identity,
                                    scale=rstdA[:, gi:gi + 1],
                                    bias=bias2A[:, gi:gi + 1]),
                                sem=semS, waits=[x_evt, cmb_free[0]])

                    def comb(e, psY=psY, sl=sl, j=j):
                        vg(e, e.tensor_tensor(s1stg, xnstg,
                                              b_outB[:, j * 512:(j + 1) * 512],
                                              ALU.add))
                        vg(e, e.tensor_tensor(s1stg, s1stg, psY, ALU.add))
                        vg(e, e.tensor_tensor(s1stg, s1stg, gstg, ALU.mult))
                        return e.tensor_tensor(ostg[sl], s1stg, x_ij[sl],
                                               ALU.add)
                    vv2 = evac("vector", psY, comb, semV, waits=[sS, sS2])
                    xij_free[sl] = vv2
                    cmb_free[0] = vv2
                    odone = dma("sync",
                                out[gi * 128:(gi + 1) * 128,
                                    j * 512:(j + 1) * 512],
                                ostg[sl], outS[sl], waits=[vv2])
                    bank_free[id(psY)] = odone
                qt_cons[0] = last_bank
            wo_cons[0] = last_bank

    for h in range(2):
        f1(h)
        f2c(h)

    out_tot = [(s, s.v) for s in outS]
    spill_tot = [(s, s.v) for s in [sxnS, sqtS[0], sqtS[1], swgS[0], swgS[1],
                                    saS] + swdS]

    def fin(e):
        for (s, v) in out_tot + spill_tot:
            if v > 0:
                e.wait_ge(s.h, v)
    plan["sync"].append(fin)

    with nc.Block() as block:
        @block.sync
        def _(e):
            for f in plan["sync"]:
                f(e)

        @block.gpsimd
        def _(e):
            for f in plan["gpsimd"]:
                f(e)

        @block.vector
        def _(e):
            for f in plan["vector"]:
                f(e)

        @block.scalar
        def _(e):
            for f in plan["scalar"]:
                f(e)

        @block.tensor
        def _(e):
            for f in plan["tensor"]:
                f(e)

    return nc


def _host_consts():
    i = np.arange(0, ROT, 2, dtype=np.float64) / ROT
    inv = 1.0 / np.power(10000.0, i)
    t = np.arange(S, dtype=np.float64)[:, None]
    fr = t * inv[None, :]
    ropc_np = np.cos(fr).astype(np.float32)
    rops_np = np.sin(fr).astype(np.float32)
    ident = np.eye(128, dtype=np.float32)
    return ropc_np, rops_np, ident


def kernel(L0_attn_X, L0_attn_W_in, L0_attn_b_in, L0_attn_W_out, L0_attn_b_out,
           L0_attn_gamma, L0_attn_beta, L0_ffn_wq_gate, L0_ffn_s_gate,
           L0_ffn_wq_up, L0_ffn_s_up, L0_ffn_wq_down, L0_ffn_s_down):
    ropc_np, rops_np, ident = _host_consts()
    W_in = np.asarray(L0_attn_W_in, dtype=np.float32)
    W_out = np.asarray(L0_attn_W_out, dtype=np.float32)
    Wg = np.asarray(L0_ffn_wq_gate, dtype=np.int32)
    Wu = np.asarray(L0_ffn_wq_up, dtype=np.int32)
    Wd = np.asarray(L0_ffn_wq_down, dtype=np.int32)
    C = lambda a: np.ascontiguousarray(a)
    shared = dict(
        wq4=C(W_in[:, :D].reshape(16, 128, 16, 128).transpose(2, 1, 0, 3)),
        wk4=C(W_in[:, D:2 * D].reshape(16, 128, 4, 512).transpose(2, 1, 0, 3)),
        wv4=C(W_in[:, 2 * D:].reshape(16, 128, 4, 512).transpose(2, 1, 0, 3)),
        wo4=C(W_out.reshape(16, 128, 4, 512).transpose(2, 1, 0, 3)),
        wg4=C(Wg.reshape(44, 128, 16, 128).transpose(0, 3, 2, 1)),
        wu4=C(Wu.reshape(44, 128, 16, 128).transpose(0, 3, 2, 1)),
        wd4=C(Wd.reshape(4, 512, 44, 128).transpose(0, 3, 2, 1)),
        b_in=C(np.asarray(L0_attn_b_in, dtype=np.float32)),
        b_out=C(np.asarray(L0_attn_b_out, dtype=np.float32)),
        gamma=C(np.asarray(L0_attn_gamma, dtype=np.float32)),
        beta=C(np.asarray(L0_attn_beta, dtype=np.float32)),
        s_g=C(np.asarray(L0_ffn_s_gate, dtype=np.float32)),
        s_u=C(np.asarray(L0_ffn_s_up, dtype=np.float32)),
        s_d=C(np.asarray(L0_ffn_s_down, dtype=np.float32)),
        ropc=ropc_np, rops=rops_np, ident=ident,
    )
    X = np.asarray(L0_attn_X, dtype=np.float32)
    nc = build_program()
    in_maps = []
    for c in range(8):
        m = dict(shared)
        m["x"] = np.ascontiguousarray(X[c])
        in_maps.append(m)
    res = run_bass_kernel_spmd(nc, in_maps, core_ids=list(range(8)))
    return np.stack([res.results[c]["out"] for c in range(8)], axis=0)


if __name__ == "__main__":
    nc = build_program()
    print("built ok, instructions:", nc.n_instructions())


# version 30
# speedup vs baseline: 1.1985x; 1.0104x over previous
import numpy as np
import ml_dtypes
import concourse.bass as bass
import concourse.mybir as mybir
from concourse.bass_utils import run_bass_kernel_spmd

F32 = mybir.dt.float32
BF16 = mybir.dt.bfloat16
I32 = mybir.dt.int32
AF = mybir.ActivationFunctionType
ALU = mybir.AluOpType

S = 2048
D = 2048
HID = 5632
NKT = D // 128
NHT = HID // 128
NT = S // 128
ROT = 1024
EPS = 1e-5
KB = 1024


class Sem:
    def __init__(self, nc, name):
        self.h = nc.alloc_semaphore(name)
        self.v = 0

    def inc(self, by=1):
        self.v += by
        return self.v


def _flat(waits):
    out = []
    for w in waits:
        if w is None:
            continue
        if isinstance(w, list):
            out.extend(x for x in w if x is not None)
        else:
            out.append(w)
    return out


def build_program():
    nc = bass.Bass("TRN2", target_bir_lowering=False)

    x_in = nc.dram_tensor("x", [S, D], F32, kind="ExternalInput").ap()
    wq4 = nc.dram_tensor("wq4", [NKT, 128, NKT, 128], F32, kind="ExternalInput").ap()
    wk4 = nc.dram_tensor("wk4", [4, 128, NKT, 512], F32, kind="ExternalInput").ap()
    wv4 = nc.dram_tensor("wv4", [4, 128, NKT, 512], F32, kind="ExternalInput").ap()
    b_in = nc.dram_tensor("b_in", [3 * D], F32, kind="ExternalInput").ap()
    wo4 = nc.dram_tensor("wo4", [4, 128, NKT, 512], F32, kind="ExternalInput").ap()
    b_out = nc.dram_tensor("b_out", [D], F32, kind="ExternalInput").ap()
    gamma = nc.dram_tensor("gamma", [D], F32, kind="ExternalInput").ap()
    beta = nc.dram_tensor("beta", [D], F32, kind="ExternalInput").ap()
    wg4 = nc.dram_tensor("wg4", [NHT, 128, NKT, 128], I32, kind="ExternalInput").ap()
    wu4 = nc.dram_tensor("wu4", [NHT, 128, NKT, 128], I32, kind="ExternalInput").ap()
    wd4 = nc.dram_tensor("wd4", [4, 128, NHT, 512], I32, kind="ExternalInput").ap()
    s_g = nc.dram_tensor("s_g", [HID], F32, kind="ExternalInput").ap()
    s_u = nc.dram_tensor("s_u", [HID], F32, kind="ExternalInput").ap()
    s_d = nc.dram_tensor("s_d", [D], F32, kind="ExternalInput").ap()
    ropc = nc.dram_tensor("ropc", [S, ROT // 2], F32, kind="ExternalInput").ap()
    rops = nc.dram_tensor("rops", [S, ROT // 2], F32, kind="ExternalInput").ap()
    ident_in = nc.dram_tensor("ident", [128, 128], F32, kind="ExternalInput").ap()

    out = nc.dram_tensor("out", [S, D], F32, kind="ExternalOutput").ap()

    qts = nc.dram_tensor("qts", [NKT, 128, S], BF16, kind="Internal").ap()
    xnts = nc.dram_tensor("xnts", [NKT, 128, S], BF16, kind="Internal").ap()
    wgs4 = nc.dram_tensor("wgs4", [NHT, 128, NKT, 128], BF16, kind="Internal").ap()
    wus4 = nc.dram_tensor("wus4", [NHT, 128, NKT, 128], BF16, kind="Internal").ap()
    wds4 = nc.dram_tensor("wds4", [4, 128, NHT, 512], BF16, kind="Internal").ap()
    a_scr = nc.dram_tensor("a_scr", [1, D], F32, kind="Internal").ap()

    def sb_at(name, shape, dtype, at):
        h = nc.alloc_sbuf_tensor_at(name, shape, dtype, offset=at)
        per_part = int(np.prod(shape[1:])) * mybir.dt.size(dtype)
        return h.ap(), at + per_part

    class Region:
        def __init__(self, start):
            self.ptr = start

        def al(self, name, shape, dtype):
            at = (self.ptr + 31) // 32 * 32
            ap, end = sb_at(name, shape, dtype, at)
            self.ptr = end
            return ap

    SBUF_BASE = 16512
    LIMIT = 229344
    G = Region(SBUF_BASE)
    idtf = G.al("idtf", [128, 128], F32)
    idtb = G.al("idtb", [128, 128], BF16)
    binP = G.al("binP", [128, 48], F32)
    sgP = G.al("sgP", [128, NHT], F32)
    suP = G.al("suP", [128, NHT], F32)
    gamT = G.al("gamT", [128, NKT], F32)
    betT = G.al("betT", [128, NKT], F32)
    mnegA = G.al("mnegA", [128, NT], F32)
    rstdA = G.al("rstdA", [128, NT], F32)
    bias2A = G.al("bias2A", [128, NT], F32)
    r2A = G.al("r2A", [128, NT], F32)
    ssqKA = G.al("ssqKA", [128, 8], F32)
    ssqVA = G.al("ssqVA", [128, 8], F32)
    wvec = G.al("wvec", [128, 8], BF16)
    tmpA = G.al("tmpA", [128, 8], F32)
    tmp1A = G.al("tmp1A", [128, 4], F32)
    epsP = G.al("epsP", [128, 1], F32)
    a_T = G.al("a_T", [128, NKT], F32)
    bnst = [G.al("bnst0", [128, 24], F32), G.al("bnst1", [128, 24], F32)]
    mvst = [G.al("mvst0", [128, 2], F32), G.al("mvst1", [128, 2], F32)]
    GLOBAL_END = G.ptr

    P = Region(GLOBAL_END)
    xn2T = P.al("xn2T", [128, NKT, KB], BF16)
    pbuf = P.al("pbuf", [128, 8, D], BF16)
    binBK = P.al("binBK", [128, D], F32)
    binBV = P.al("binBV", [128, D], F32)
    _xoff = (P.ptr + 31) // 32 * 32
    x_t = [P.al("x_t0", [128, D], F32), P.al("x_t1", [128, D], F32)]
    _xnoff = (P.ptr + 31) // 32 * 32
    xn_t = [P.al("xn_t0", [128, D], F32), P.al("xn_t1", [128, D], F32)]
    xnr2b = [P.al(f"xnr2b{i}", [128, D], BF16) for i in range(4)]
    kvbK2, _ = sb_at("kvbK2", [128, NKT, 512], BF16, _xoff)
    kvbV2, _ = sb_at("kvbV2", [128, NKT, 512], BF16, _xnoff)
    xntg = P.al("xntg", [128, NKT, 256], BF16)
    a_row = P.al("a_row", [1, D], F32)
    kvbK = P.al("kvbK", [128, NKT, 512], BF16)
    kvbV = P.al("kvbV", [128, NKT, 512], BF16)
    wqB = [P.al("wqB0", [128, NKT, 128], BF16), P.al("wqB1", [128, NKT, 128], BF16)]
    qtstg = [P.al("qtstg0", [128, 512], BF16), P.al("qtstg1", [128, 512], BF16)]
    krst = [P.al("krst0", [128, 512], F32), P.al("krst1", [128, 512], F32)]
    vrst = [P.al("vrst0", [128, 512], F32), P.al("vrst1", [128, 512], F32)]
    sqscr = P.al("sqscr", [128, 512], F32)
    vscr = P.al("vscr", [128, 512], F32)
    ropcB = [P.al("ropcB0", [128, 256], F32), P.al("ropcB1", [128, 256], F32)]
    ropsB = [P.al("ropsB0", [128, 256], F32), P.al("ropsB1", [128, 256], F32)]
    assert P.ptr <= LIMIT, f"P1 over SBUF: {P.ptr}"

    F = Region(GLOBAL_END)
    hT = F.al("hT", [128, NHT, KB], BF16)
    HT_END = F.ptr
    xnT = F.al("xnT", [128, NKT, KB], BF16)
    wgB = [F.al("wgB0", [128, NKT, 128], BF16), F.al("wgB1", [128, NKT, 128], BF16)]
    wuB = [F.al("wuB0", [128, NKT, 128], BF16), F.al("wuB1", [128, NKT, 128], BF16)]
    sigstg = F.al("sigstg", [128, 512], F32)
    upstg = F.al("upstg", [128, 512], F32)
    assert F.ptr <= LIMIT, f"F1 over SBUF: {F.ptr}"

    F2 = Region(HT_END)
    wdB = F2.al("wdB", [128, NHT, 512], BF16)
    woB = F2.al("woB", [128, NKT, 512], BF16)
    qtg = F2.al("qtg", [128, NKT, 512], BF16)
    b_outB = F2.al("b_outB", [128, D], F32)
    s_dB = F2.al("s_dB", [128, D], F32)
    x_ij = [F2.al("x_ij0", [128, 512], F32), F2.al("x_ij1", [128, 512], F32)]
    ffnstg = F2.al("ffnstg", [128, 512], F32)
    gstg = F2.al("gstg", [128, 512], F32)
    xnstg = F2.al("xnstg", [128, 512], F32)
    s1stg = F2.al("s1stg", [128, 512], F32)
    ostg = [F2.al("ostg0", [128, 512], F32), F2.al("ostg1", [128, 512], F32)]
    assert F2.ptr <= LIMIT, f"F2C over SBUF: {F2.ptr}"

    psX = [nc.alloc_psum_tensor(f"psX{i}", [128, 512], F32).ap() for i in range(4)]
    psT = [nc.alloc_psum_tensor(f"psT{i}", [128, 512], F32).ap() for i in range(2)]
    psA = nc.alloc_psum_tensor("psA", [128, 512], F32).ap()

    dmaI = Sem(nc, "dmaI")
    xldS = [Sem(nc, "xld0"), Sem(nc, "xld1")]
    ropeS = [Sem(nc, "rope0"), Sem(nc, "rope1")]
    sxnS = Sem(nc, "sxn")
    wqS = [Sem(nc, "wq0"), Sem(nc, "wq1")]
    sqtS = [Sem(nc, "sqt0"), Sem(nc, "sqt1")]
    kvS = [Sem(nc, "kv0"), Sem(nc, "kv1")]
    saS = Sem(nc, "saS")
    aTS = Sem(nc, "aTS")
    bsS = Sem(nc, "bsS")
    xnlS = Sem(nc, "xnlS")
    wguS = [Sem(nc, "wgu0"), Sem(nc, "wgu1")]
    wgaS = [Sem(nc, "wga0"), Sem(nc, "wga1")]
    wglS = [Sem(nc, "wgl0"), Sem(nc, "wgl1")]
    swgS = [Sem(nc, "swg0"), Sem(nc, "swg1")]
    wdlS = [Sem(nc, f"wdl{s}") for s in range(4)]
    wdl2S = [Sem(nc, f"wdm{s}") for s in range(4)]
    swdS = [Sem(nc, f"swd{s}") for s in range(4)]
    woS = Sem(nc, "woS")
    qtlS = Sem(nc, "qtlS")
    xijS = [Sem(nc, "xij0"), Sem(nc, "xij1")]
    ibS = Sem(nc, "ibS")
    outS = [Sem(nc, "out0"), Sem(nc, "out1")]
    semT = Sem(nc, "semT")
    semV = Sem(nc, "semV")
    semS = Sem(nc, "semS")
    semG = Sem(nc, "semG")
    semW = Sem(nc, "semW")
    wctr = [0]

    def vg(e, ins):
        wctr[0] += 1
        ins.then_inc(semW.h, 1)
        e.wait_ge(semW.h, wctr[0])

    semY = Sem(nc, "semY")
    yctr = [0]

    def sg(e, ins):
        yctr[0] += 1
        ins.then_inc(semY.h, 1)
        e.wait_ge(semY.h, yctr[0])

    plan = {"sync": [], "gpsimd": [], "vector": [], "scalar": [], "tensor": []}

    def dma(q, out_ap, in_ap, sem, waits=(), slow=False):
        wl = _flat(waits)

        def f(e, wl=wl, out_ap=out_ap, in_ap=in_ap, sem=sem, slow=slow):
            for (s, v) in wl:
                if v > 0:
                    e.wait_ge(s.h, v)
            e.dma_start(out_ap, in_ap,
                        allow_slow_non_contiguous=slow).then_inc(sem.h, 16)

        plan[q].append(f)
        sem.inc(16)
        return (sem, sem.v)

    eng_chain = {"vector": semV, "scalar": semS, "gpsimd": semG}

    def oneng(eng, fn, sem=None, waits=()):
        ch = eng_chain.get(eng)
        if ch is not None:
            assert sem is None or sem is ch
            sem = ch
        wl = _flat(list(waits) + ([(ch, ch.v)] if ch is not None else []))
        byid = {}
        for (s, v) in wl:
            if v > byid.get(id(s), (None, 0))[1]:
                byid[id(s)] = (s, v)
        wl = list(byid.values())
        if sem is not None:
            sem.inc(1)

        def f(e, wl=wl, fn=fn, sem=sem):
            for (s, v) in wl:
                if v > 0:
                    e.wait_ge(s.h, v)
            ins = fn(e)
            if sem is not None:
                assert ins is not None, "producer fn must return its last instr"
                ins.then_inc(sem.h, 1)

        plan[eng].append(f)
        return (sem, sem.v) if sem is not None else None

    bank_free = {}
    bank_ready = {}

    def mm_group(ps, mms, waits=(), start=True, stop=True):
        wl = _flat(list(waits) + ([bank_free.get(id(ps))] if start else []))

        def f(e, wl=wl, mms=mms, start=start, stop=stop):
            for (s, v) in wl:
                if v > 0:
                    e.wait_ge(s.h, v)
            n = len(mms)
            ins = None
            for i, (l, r, o) in enumerate(mms):
                ins = e.matmul(o if o is not None else ps, l, r,
                               start=(start and i == 0),
                               stop=(stop and i == n - 1))
            ins.then_inc(semT.h, 1)

        plan["tensor"].append(f)
        semT.inc(1)
        evt = (semT, semT.v)
        bank_ready[id(ps)] = evt
        return evt

    def tp_group(ps, items, waits=()):
        wl = _flat(list(waits) + [bank_free.get(id(ps))])

        def f(e, wl=wl, items=items):
            for (s, v) in wl:
                if v > 0:
                    e.wait_ge(s.h, v)
            ins = None
            for (src, o) in items:
                ins = e.transpose(o, src, idtf)
            ins.then_inc(semT.h, 1)

        plan["tensor"].append(f)
        semT.inc(1)
        evt = (semT, semT.v)
        bank_ready[id(ps)] = evt
        return evt

    def evac(eng_name, ps, fn, sem, waits=()):
        evt = oneng(eng_name, fn, sem=sem,
                    waits=list(waits) + [bank_ready[id(ps)]])
        bank_free[id(ps)] = evt
        return evt

    dma("sync", idtf, ident_in, dmaI)
    idtb_done = dma("gpsimd", idtb, ident_in, ibS)
    dma("sync", binP, b_in.rearrange("(m p) -> p m", p=128), dmaI, slow=True)
    dma("sync", sgP, s_g.rearrange("(m p) -> p m", p=128), dmaI, slow=True)
    dma("sync", suP, s_u.rearrange("(m p) -> p m", p=128), dmaI, slow=True)
    dma("sync", gamT, gamma.rearrange("(k p) -> p k", p=128), dmaI, slow=True)
    dma("sync", betT, beta.rearrange("(k p) -> p k", p=128), dmaI, slow=True)
    dma("sync", binBK, b_in[D:2 * D].rearrange("(o d) -> o d", o=1).broadcast_to([128, D]), dmaI)
    init_done = dma("sync", binBV,
                    b_in[2 * D:3 * D].rearrange("(o d) -> o d", o=1).broadcast_to([128, D]),
                    dmaI)
    a_zero_evt = oneng("vector", lambda e: e.memset(a_row, 0.0), sem=semV,
                       waits=[init_done])
    eps_evt = oneng("vector", lambda e: e.memset(epsP, EPS), sem=semV)
    def load_set6(e):
        return e.add_instruction(mybir.InstLoadActFuncSet(
            name=nc.get_next_instruction_name(), act_func_set_id=6,
            ins=[], outs=[]))
    oneng("scalar", load_set6, waits=[init_done])

    x_cons = [None, None]
    xn_cons = [None, None]
    xr2b_evt = [None] * 4
    xr2b_cons = [None]
    bnst_free = [None, None]
    xntg_free = [None]
    pend_spill = [None]
    xn2T_guard = [None]
    kv2_guard = [None]

    def p1_stepA_tile(h, il):
        gi = h * 8 + il
        par = gi % 2
        sl = il % 2
        quad = il % 4
        tb = par * 4
        lnc = tmpA[:, tb + 0:tb + 1]
        t2 = tmpA[:, tb + 1:tb + 2]

        alias_g = [kv2_guard[0]] if (h == 1 and il < 2) else []
        x_evt = dma("sync", x_t[par], x_in[gi * 128:(gi + 1) * 128, :],
                    xldS[par], waits=[x_cons[par]] + alias_g)
        if pend_spill[0] is not None:
            g2d, evd = pend_spill[0]
            pend_spill[0] = None
            last = None
            for part in range(4):
                last = dma("sync",
                           xnts[part * 4:(part + 1) * 4, :,
                                g2d * 256:(g2d + 1) * 256]
                           .rearrange("k p t -> p k t"),
                           xntg[:, part * 4:(part + 1) * 4, :],
                           sxnS, waits=evd if part == 0 else [])
            xntg_free[0] = last

        def bnf(e, par=par):
            ins = None
            for c in range(4):
                ins = e.bn_stats(bnst[par][:, c * 6:(c + 1) * 6],
                                 x_t[par][:, c * 512:(c + 1) * 512])
                if c < 3:
                    vg(e, ins)
            return ins
        bn_done = oneng("vector", bnf, sem=semV,
                        waits=[x_evt, bnst_free[par]])
        agg_done = oneng("vector",
                         lambda e, par=par: e.bn_aggr(mvst[par], bnst[par]),
                         sem=semV, waits=[bn_done])
        bnst_free[par] = agg_done

        ev1 = oneng("vector",
                    lambda e, par=par, gi=gi: e.tensor_scalar(
                        mnegA[:, gi:gi + 1], mvst[par][:, 0:1],
                        -1.0, None, ALU.mult),
                    sem=semV, waits=[agg_done])

        def r1f(e, par=par, gi=gi, lnc=lnc):
            sg(e, e.activation(lnc, mvst[par][:, 1:2], AF.Ln, bias=epsP))
            return e.activation(rstdA[:, gi:gi + 1], lnc, AF.Exp, scale=-0.5)
        sc1 = oneng("scalar", r1f, sem=semS, waits=[agg_done, eps_evt])

        def fn2(e, par=par, gi=gi, t2=t2):
            vg(e, e.tensor_tensor(bias2A[:, gi:gi + 1], mnegA[:, gi:gi + 1],
                                  rstdA[:, gi:gi + 1], ALU.mult))
            vg(e, e.tensor_tensor(t2, mvst[par][:, 1:2],
                                  rstdA[:, gi:gi + 1], ALU.mult))
            return e.tensor_tensor(t2, t2, rstdA[:, gi:gi + 1], ALU.mult)
        ev2 = oneng("vector", fn2, sem=semV, waits=[sc1, ev1])

        def r2f(e, gi=gi, lnc=lnc, t2=t2):
            sg(e, e.activation(lnc, t2, AF.Ln, bias=epsP))
            return e.activation(r2A[:, gi:gi + 1], lnc, AF.Exp, scale=-0.5)
        sc2 = oneng("scalar", r2f, sem=semS, waits=[ev2])

        xn_evt = oneng("gpsimd",
                       lambda e, par=par, gi=gi: e.tensor_scalar(
                           xn_t[par], x_t[par], mnegA[:, gi:gi + 1],
                           rstdA[:, gi:gi + 1], ALU.add, ALU.mult),
                       sem=semG, waits=[x_evt, ev1, sc1, xn_cons[par]] + alias_g)
        x_cons[par] = [bn_done, xn_evt]

        xr2_evt = oneng("scalar",
                        lambda e, par=par, gi=gi, quad=quad: e.activation(
                            xnr2b[quad], xn_t[par], AF.Identity,
                            scale=r2A[:, gi:gi + 1]),
                        sem=semS, waits=[xn_evt, sc2, xr2b_cons[0]])
        xr2b_evt[quad] = xr2_evt

        last_tp = None
        for g in range(4):
            ps = psT[g % 2]
            items = [(xn_t[par][:, (4 * g + q) * 128:(4 * g + q + 1) * 128],
                      ps[:, q * 128:(q + 1) * 128]) for q in range(4)]
            last_tp = tp_group(ps, items, waits=[xn_evt, init_done])
            ew = [xntg_free[0]] if (sl == 0 and g == 0) else []

            def ev(e, ps=ps, g=g, sl=sl):
                return e.tensor_copy(
                    xntg[:, 4 * g:4 * (g + 1), sl * 128:(sl + 1) * 128],
                    ps.rearrange("p (q t) -> p q t", q=4))
            evac("vector", ps, ev, semV, waits=ew)
        xn_cons[par] = [xr2_evt, last_tp]

        if sl == 1:
            pend_spill[0] = (gi // 2, [(semV, semV.v), (semS, semS.v)])

        if quad == 3:
            a4 = il // 4
            last_tp2 = None
            for kt in range(NKT):
                ps = psX[kt % 4]
                mms = [(xnr2b[q][:, kt * 128:(kt + 1) * 128], idtb,
                        ps[:, q * 128:(q + 1) * 128]) for q in range(4)]
                w0 = list(xr2b_evt) + [idtb_done]
                last_tp2 = mm_group(ps, mms, waits=w0)
                gw = [xn2T_guard[0]] if (h == 1 and a4 == 0 and kt < 4) else []

                def fxs(e, ps=ps, kt=kt, a4=a4):
                    return e.activation(
                        xn2T[:, kt, a4 * 512:(a4 + 1) * 512], ps, AF.Identity,
                        scale=gamT[:, kt:kt + 1], bias=betT[:, kt:kt + 1])
                evac("scalar", ps, fxs, semS, waits=gw)
            xr2b_cons[0] = last_tp2

    wq_cons = [None, None]
    stg_free = [None, None]

    def p1_stepB(h, a_done):
        for m in range(NKT):
            sl = m % 2
            w_evt = dma("gpsimd", wqB[sl], wq4[m],
                        wqS[sl],
                        waits=[wq_cons[sl]] + ([init_done] if m == 0 else []))
            last_grp = None
            for n in range(2):
                ps = psX[(2 * m + n) % 4]
                mms = [(wqB[sl][:, k, :], xn2T[:, k, n * 512:(n + 1) * 512], None)
                       for k in range(NKT)]
                extra = a_done if (m == 0 and n == 0) else []
                last_grp = mm_group(ps, mms, waits=[w_evt] + extra)
                st = qtstg[n]

                def ev(e, ps=ps, st=st, m=m):
                    return e.activation(st, ps, AF.Identity,
                                        bias=binP[:, m:m + 1])
                vv = evac("scalar", ps, ev, semS, waits=[stg_free[n]])
                stg_free[n] = dma("sync",
                                  qts[m, :, h * KB + n * 512:h * KB + (n + 1) * 512],
                                  st, sqtS[n], waits=[vv])
            wq_cons[sl] = last_grp

    kv_cons = [None, None]
    rope_cons = [None, None]

    def p1_stepC(h, a_eng_done):
        k_free = [None, None]
        v_free = [None, None]
        pbuf_guard = (semT, semT.v)
        z1 = oneng("vector", lambda e: e.memset(ssqKA, 0.0), sem=semV,
                   waits=[pbuf_guard, init_done])
        z2 = oneng("vector", lambda e: e.memset(ssqVA, 0.0), sem=semV)
        zevt = z2
        for c in range(4):
            slot = c % 2
            bufK = kvbK if slot == 0 else kvbK2
            bufV = kvbV if slot == 0 else kvbV2
            w0 = [kv_cons[slot]]
            if c == 1:
                w0 = w0 + a_eng_done
            dma("gpsimd", bufK, wk4[c], kvS[slot], waits=w0)
            kv_evt = dma("gpsimd", bufV, wv4[c], kvS[slot], waits=[])
            for il in range(8):
                gi = h * 8 + il
                rsl = il % 2
                kb = krst[rsl]
                vb = vrst[rsl]
                rope_evt = None
                if c < 2:
                    dma("sync", ropcB[rsl],
                        ropc[gi * 128:(gi + 1) * 128, c * 256:(c + 1) * 256],
                        ropeS[rsl], waits=[rope_cons[rsl]])
                    rope_evt = dma("sync", ropsB[rsl],
                                   rops[gi * 128:(gi + 1) * 128,
                                        c * 256:(c + 1) * 256],
                                   ropeS[rsl], waits=[])
                psK = psX[(2 * il) % 4]
                psV = psX[(2 * il + 1) % 4]
                mm_group(psK, [(xn2T[:, k, il * 128:(il + 1) * 128],
                                bufK[:, k, :], None) for k in range(NKT)],
                         waits=[kv_evt])
                mm_group(psV, [(xn2T[:, k, il * 128:(il + 1) * 128],
                                bufV[:, k, :], None) for k in range(NKT)],
                         waits=[kv_evt])
                if il == 7:
                    kv_cons[slot] = bank_ready[id(psV)]

                def kpath(e, psK=psK, c=c, rsl=rsl, kb=kb):
                    ins = e.tensor_tensor(kb, psK,
                                          binBK[:, c * 512:(c + 1) * 512], ALU.add)
                    if c < 2:
                        vg(e, ins)
                        cs, sn = ropcB[rsl], ropsB[rsl]
                        ke = kb[:, 0:512:2]
                        ko = kb[:, 1:512:2]
                        vg(e, e.tensor_tensor(vscr[:, 0:256], ke, sn, ALU.mult))
                        vg(e, e.tensor_tensor(vscr[:, 256:512], ko, sn, ALU.mult))
                        vg(e, e.tensor_tensor(ke, ke, cs, ALU.mult))
                        vg(e, e.tensor_tensor(ko, ko, cs, ALU.mult))
                        vg(e, e.tensor_tensor(ke, ke, vscr[:, 256:512], ALU.subtract))
                        ins = e.tensor_tensor(ko, ko, vscr[:, 0:256], ALU.add)
                    return ins
                kw = [bank_ready[id(psK)], zevt, k_free[rsl]]
                if c < 2:
                    kw.append(rope_evt)
                vvK = oneng("vector", kpath, sem=semV, waits=kw)
                bank_free[id(psK)] = vvK
                sS = oneng("scalar",
                           lambda e, kb=kb: e.activation(
                               sqscr, kb, AF.Square, accum_out=tmp1A[:, 0:1]),
                           sem=semS, waits=[vvK])
                k_free[rsl] = sS
                vv2 = oneng("vector",
                            lambda e, il=il: e.tensor_tensor(
                                ssqKA[:, il:il + 1], ssqKA[:, il:il + 1],
                                tmp1A[:, 0:1], ALU.add),
                            sem=semV, waits=[sS])

                def vpath(e, psV=psV, c=c, rsl=rsl, vb=vb):
                    ins = e.tensor_tensor(vb, psV,
                                          binBV[:, c * 512:(c + 1) * 512], ALU.add)
                    if c < 2:
                        vg(e, ins)
                        cs, sn = ropcB[rsl], ropsB[rsl]
                        ve_ = vb[:, 0:512:2]
                        vo = vb[:, 1:512:2]
                        vg(e, e.tensor_tensor(vscr[:, 0:256], ve_, sn, ALU.mult))
                        vg(e, e.tensor_tensor(vscr[:, 256:512], vo, sn, ALU.mult))
                        vg(e, e.tensor_tensor(ve_, ve_, cs, ALU.mult))
                        vg(e, e.tensor_tensor(vo, vo, cs, ALU.mult))
                        vg(e, e.tensor_tensor(ve_, ve_, vscr[:, 256:512], ALU.subtract))
                        ins = e.tensor_tensor(vo, vo, vscr[:, 0:256], ALU.add)
                    return ins
                vw = [bank_ready[id(psV)], v_free[rsl]]
                if c < 2:
                    vw.append(rope_evt)
                vvV = oneng("vector", vpath, sem=semV, waits=vw)
                bank_free[id(psV)] = vvV
                if c < 2:
                    rope_cons[rsl] = vvV
                sS2 = oneng("scalar",
                            lambda e, vb=vb: e.activation(
                                sqscr, vb, AF.Square, accum_out=tmp1A[:, 1:2]),
                            sem=semS, waits=[vvV])
                vv4 = oneng("vector",
                            lambda e, il=il: e.tensor_tensor(
                                ssqVA[:, il:il + 1], ssqVA[:, il:il + 1],
                                tmp1A[:, 1:2], ALU.add),
                            sem=semV, waits=[sS2])
                pv = oneng("vector",
                           lambda e, il=il, c=c, kb=kb, vb=vb: e.tensor_tensor(
                               pbuf[:, il, c * 512:(c + 1) * 512], kb, vb,
                               ALU.mult),
                           sem=semV, waits=[vv2, vv4])
                v_free[rsl] = [sS2, pv]

        wlast = [None, None]
        for il in range(8):
            p2 = il % 2
            vv = oneng("vector",
                       lambda e, il=il, p2=p2: e.tensor_tensor(
                           tmp1A[:, p2:p2 + 1], ssqKA[:, il:il + 1],
                           ssqVA[:, il:il + 1], ALU.mult),
                       sem=semV, waits=[wlast[p2]])

            def wf(e, il=il, p2=p2):
                sg(e, e.activation(tmp1A[:, 2 + p2:3 + p2],
                                   tmp1A[:, p2:p2 + 1], AF.Ln))
                return e.activation(wvec[:, il:il + 1],
                                    tmp1A[:, 2 + p2:3 + p2], AF.Exp,
                                    scale=-0.5)
            wlast[p2] = oneng("scalar", wf, sem=semS, waits=[vv])
        w_ready = [(semS, semS.v)]
        for c2 in range(4):
            mms = [(wvec[:, il:il + 1], pbuf[:, il, c2 * 512:(c2 + 1) * 512],
                    psA[0:1, :]) for il in range(8)]
            mm_group(psA, mms, waits=w_ready + [a_zero_evt])

            def aev(e, c2=c2):
                return e.tensor_tensor(a_row[0:1, c2 * 512:(c2 + 1) * 512],
                                       a_row[0:1, c2 * 512:(c2 + 1) * 512],
                                       psA[0:1, :], ALU.add)
            evac("vector", psA, aev, semV)

    for h in range(2):
        for il in range(8):
            p1_stepA_tile(h, il)
        if pend_spill[0] is not None:
            g2d, evd = pend_spill[0]
            pend_spill[0] = None
            last = None
            for part in range(4):
                last = dma("sync",
                           xnts[part * 4:(part + 1) * 4, :,
                                g2d * 256:(g2d + 1) * 256]
                           .rearrange("k p t -> p k t"),
                           xntg[:, part * 4:(part + 1) * 4, :],
                           sxnS, waits=evd if part == 0 else [])
            xntg_free[0] = last
        a_done = [(semS, semS.v), (semV, semV.v)]
        a_eng_done = [(semS, semS.v), (semV, semV.v), (semT, semT.v),
                      (semG, semG.v)]
        p1_stepB(h, a_done)
        p1_stepC(h, a_eng_done)
        kv2_guard[0] = kv_cons[1]
        xn2T_guard[0] = (semT, semT.v)

    vv = oneng("vector",
               lambda e: e.tensor_scalar(a_row, a_row, 0.5, 0.5,
                                         ALU.mult, ALU.add),
               sem=semV)
    ase = dma("sync", a_scr, a_row, saS, waits=[vv])
    aT_ready = dma("sync", a_T, a_scr.rearrange("o (k p) -> p (o k)", p=128),
                   aTS, waits=[ase], slow=True)

    p1_all_done = [(semV, semV.v), (semS, semS.v), (semT, semT.v),
                   (semG, semG.v), (sxnS, sxnS.v),
                   (sqtS[0], sqtS[0].v), (sqtS[1], sqtS[1].v)]
    dma("sync", b_outB,
        b_out.rearrange("(o d) -> o d", o=1).broadcast_to([128, D]), bsS,
        waits=p1_all_done)
    bs_loaded = dma("sync", s_dB,
                    s_d.rearrange("(o d) -> o d", o=1).broadcast_to([128, D]),
                    bsS)

    wf_cons = [None, None]
    wf_spill = [None, None]
    hs_free = [None]

    def f1(h):
        if h == 0:
            oneng("scalar", lambda e: e.preload_act_table(AF.Silu))
        f2c_guard = ([(semT, semT.v)] +
                     [(s, s.v) for s in swdS]) if h == 1 else []
        xnT_evt = None
        for part in range(4):
            w0 = []
            if part == 0:
                w0 = p1_all_done if h == 0 else f2c_guard
            xnT_evt = dma("sync", xnT[:, part * 4:(part + 1) * 4, :],
                          xnts[part * 4:(part + 1) * 4, :, h * KB:(h + 1) * KB]
                          .rearrange("k p t -> p k t"),
                          xnlS, waits=w0)
        for m in range(NHT):
            sl = m % 2
            reuse = [wf_cons[sl], wf_spill[sl]]
            if h == 0:
                ev_g = dma("gpsimd", wgB[sl], wg4[m], wgaS[sl],
                           waits=[xnT_evt] + reuse)
                ev_u = dma("gpsimd", wuB[sl], wu4[m], wguS[sl], waits=[])
                sub_g = oneng("vector",
                              lambda e, sl=sl: e.tensor_scalar(
                                  wgB[sl].rearrange("p k c -> p (k c)"),
                                  wgB[sl].rearrange("p k c -> p (k c)"),
                                  -128.0, None, ALU.add),
                              sem=semV, waits=[ev_g])
                sub_u = oneng("vector",
                              lambda e, sl=sl: e.tensor_scalar(
                                  wuB[sl].rearrange("p k c -> p (k c)"),
                                  wuB[sl].rearrange("p k c -> p (k c)"),
                                  -128.0, None, ALU.add),
                              sem=semV, waits=[ev_u])
                dma("sync", wgs4[m], wgB[sl], swgS[sl], waits=[sub_g])
                wf_spill[sl] = dma("sync", wus4[m], wuB[sl], swgS[sl],
                                   waits=[sub_u])
            else:
                dma("sync", wgB[sl], wgs4[m], wglS[sl],
                    waits=([xnT_evt] + reuse))
                w_evt = dma("sync", wuB[sl], wus4[m], wglS[sl], waits=[])
                wf_spill[sl] = None
            if h == 0:
                wg_evt, wu_evt = sub_g, sub_u
            else:
                wg_evt, wu_evt = w_evt, w_evt
            lastg = None
            for n in range(2):
                psG = psX[(2 * m + n) % 2]
                psU = psX[2 + (2 * m + n) % 2]
                mm_group(psG, [(wgB[sl][:, k, :],
                                xnT[:, k, n * 512:(n + 1) * 512], None)
                               for k in range(NKT)], waits=[wg_evt])
                lastg = mm_group(psU, [(wuB[sl][:, k, :],
                                        xnT[:, k, n * 512:(n + 1) * 512], None)
                                       for k in range(NKT)], waits=[wu_evt])
                sSe = evac("scalar", psG,
                           lambda e, psG=psG, m=m: e.activation(
                               sigstg, psG, AF.Silu, scale=sgP[:, m:m + 1]),
                           semS, waits=[hs_free[0]])
                vVe = evac("vector", psU,
                           lambda e, psU=psU, m=m: e.tensor_scalar(
                               upstg, psU, suP[:, m:m + 1], None, ALU.mult),
                           semV, waits=[hs_free[0]])
                vv2 = oneng("vector",
                            lambda e, m=m, n=n: e.tensor_tensor(
                                hT[:, m, n * 512:(n + 1) * 512], sigstg, upstg,
                                ALU.mult),
                            sem=semV, waits=[sSe, vVe])
                hs_free[0] = vv2
                bank_free[id(psG)] = vv2
                bank_free[id(psU)] = vv2
            wf_cons[sl] = lastg

    wd_seg_cons = [None] * 4
    wd_seg_spill = [None] * 4
    wo_cons = [None]
    qt_cons = [None]
    xij_free = [None, None]
    gs_free = [None]
    cmb_free = [None]
    SEGS = [(11 * s, [(11 * s, 4), (11 * s + 4, 4), (11 * s + 8, 3)])
            for s in range(4)]

    def f2c(h):
        f1_done = [(semV, semV.v), (semT, semT.v)]
        if h == 0:
            f1_done += [(swgS[0], swgS[0].v), (swgS[1], swgS[1].v)]
        for j in range(4):
            wd_seg_evt = [None] * 4
            for s, (base, parts) in enumerate(SEGS):
                w0 = [wd_seg_cons[s], wd_seg_spill[s]]
                if j == 0:
                    w0 = w0 + f1_done
                if h == 0:
                    ld = dma("gpsimd", wdB[:, base:base + 11, :],
                             wd4[j, :, base:base + 11, :],
                             wdlS[s], waits=w0)
                    sub = oneng("vector",
                                lambda e, base=base: e.tensor_scalar(
                                    wdB[:, base:base + 11, :]
                                    .rearrange("p k c -> p (k c)"),
                                    wdB[:, base:base + 11, :]
                                    .rearrange("p k c -> p (k c)"),
                                    -128.0, None, ALU.add),
                                sem=semV, waits=[ld])
                    wd_seg_spill[s] = dma("sync",
                                          wds4[j, :, base:base + 11, :],
                                          wdB[:, base:base + 11, :],
                                          swdS[s], waits=[sub])
                    wd_seg_evt[s] = sub
                else:
                    ld = dma("sync", wdB[:, base:base + 11, :],
                             wds4[j, :, base:base + 11, :],
                             wdl2S[s], waits=w0)
                    wd_seg_spill[s] = None
                    wd_seg_evt[s] = ld
            wo_evt = dma("gpsimd", woB, wo4[j], woS,
                         waits=f1_done + [wo_cons[0]])

            last_bank = None
            for g in range(2):
                qt_evt = None
                for part in range(4):
                    qt_evt = dma("sync", qtg[:, part * 4:(part + 1) * 4, :],
                                 qts[part * 4:(part + 1) * 4, :,
                                     h * KB + g * 512:h * KB + (g + 1) * 512]
                                 .rearrange("k p t -> p k t"),
                                 qtlS,
                                 waits=([aT_ready, qt_cons[0]] + f1_done)
                                 if part == 0 else [])

                def ascale(e):
                    ins = None
                    for k in range(NKT):
                        ins = e.tensor_scalar(qtg[:, k, :], qtg[:, k, :],
                                              a_T[:, k:k + 1], None, ALU.mult)
                        if k < NKT - 1:
                            vg(e, ins)
                    return ins
                aq_evt = oneng("vector", ascale, sem=semV, waits=[qt_evt])
                for i4 in range(4):
                    il = g * 4 + i4
                    gi = h * 8 + il
                    sl = il % 2
                    psD = psX[sl]
                    psY = psX[2 + sl]
                    for s, (base, _) in enumerate(SEGS):
                        ev = mm_group(psD,
                                      [(hT[:, k, il * 128:(il + 1) * 128],
                                        wdB[:, k, :], None)
                                       for k in range(base, base + 11)],
                                      waits=[wd_seg_evt[s]],
                                      start=(s == 0), stop=(s == 3))
                        wd_seg_cons[s] = ev
                    last_bank = mm_group(psY,
                                         [(qtg[:, k, i4 * 128:(i4 + 1) * 128],
                                           woB[:, k, :], None)
                                          for k in range(NKT)],
                                         waits=[aq_evt, wo_evt])
                    x_evt = dma("sync", x_ij[sl],
                                x_in[gi * 128:(gi + 1) * 128,
                                     j * 512:(j + 1) * 512],
                                xijS[sl], waits=[xij_free[sl]])
                    bsw = [bs_loaded] if (h == 0 and j == 0 and il == 0) else []
                    vv = evac("vector", psD,
                              lambda e, psD=psD, j=j: e.tensor_tensor(
                                  ffnstg, psD,
                                  s_dB[:, j * 512:(j + 1) * 512], ALU.mult),
                              semV, waits=bsw + [gs_free[0]])
                    sS = oneng("scalar",
                               lambda e: e.activation(gstg, ffnstg, AF.Silu),
                               sem=semS, waits=[vv, cmb_free[0]])
                    gs_free[0] = sS
                    sS2 = oneng("scalar",
                                lambda e, sl=sl, gi=gi: e.activation(
                                    xnstg, x_ij[sl], AF.Identity,
                                    scale=rstdA[:, gi:gi + 1],
                                    bias=bias2A[:, gi:gi + 1]),
                                sem=semS, waits=[x_evt, cmb_free[0]])

                    def comb(e, psY=psY, sl=sl, j=j):
                        vg(e, e.tensor_tensor(s1stg, xnstg,
                                              b_outB[:, j * 512:(j + 1) * 512],
                                              ALU.add))
                        vg(e, e.tensor_tensor(s1stg, s1stg, psY, ALU.add))
                        vg(e, e.tensor_tensor(s1stg, s1stg, gstg, ALU.mult))
                        return e.tensor_tensor(ostg[sl], s1stg, x_ij[sl],
                                               ALU.add)
                    vv2 = evac("vector", psY, comb, semV, waits=[sS, sS2])
                    xij_free[sl] = vv2
                    cmb_free[0] = vv2
                    odone = dma("sync",
                                out[gi * 128:(gi + 1) * 128,
                                    j * 512:(j + 1) * 512],
                                ostg[sl], outS[sl], waits=[vv2])
                    bank_free[id(psY)] = odone
                qt_cons[0] = last_bank
            wo_cons[0] = last_bank

    for h in range(2):
        f1(h)
        f2c(h)

    out_tot = [(s, s.v) for s in outS]
    spill_tot = [(s, s.v) for s in [sxnS, sqtS[0], sqtS[1], swgS[0], swgS[1],
                                    saS] + swdS]

    def fin(e):
        for (s, v) in out_tot + spill_tot:
            if v > 0:
                e.wait_ge(s.h, v)
    plan["sync"].append(fin)

    with nc.Block() as block:
        @block.sync
        def _(e):
            for f in plan["sync"]:
                f(e)

        @block.gpsimd
        def _(e):
            for f in plan["gpsimd"]:
                f(e)

        @block.vector
        def _(e):
            for f in plan["vector"]:
                f(e)

        @block.scalar
        def _(e):
            for f in plan["scalar"]:
                f(e)

        @block.tensor
        def _(e):
            for f in plan["tensor"]:
                f(e)

    return nc


def _host_consts():
    i = np.arange(0, ROT, 2, dtype=np.float64) / ROT
    inv = 1.0 / np.power(10000.0, i)
    t = np.arange(S, dtype=np.float64)[:, None]
    fr = t * inv[None, :]
    ropc_np = np.cos(fr).astype(np.float32)
    rops_np = np.sin(fr).astype(np.float32)
    ident = np.eye(128, dtype=np.float32)
    return ropc_np, rops_np, ident


def kernel(L0_attn_X, L0_attn_W_in, L0_attn_b_in, L0_attn_W_out, L0_attn_b_out,
           L0_attn_gamma, L0_attn_beta, L0_ffn_wq_gate, L0_ffn_s_gate,
           L0_ffn_wq_up, L0_ffn_s_up, L0_ffn_wq_down, L0_ffn_s_down):
    ropc_np, rops_np, ident = _host_consts()
    W_in = np.asarray(L0_attn_W_in, dtype=np.float32)
    W_out = np.asarray(L0_attn_W_out, dtype=np.float32)
    Wg = np.asarray(L0_ffn_wq_gate, dtype=np.int32)
    Wu = np.asarray(L0_ffn_wq_up, dtype=np.int32)
    Wd = np.asarray(L0_ffn_wq_down, dtype=np.int32)
    C = lambda a: np.ascontiguousarray(a)
    shared = dict(
        wq4=C(W_in[:, :D].reshape(16, 128, 16, 128).transpose(2, 1, 0, 3)),
        wk4=C(W_in[:, D:2 * D].reshape(16, 128, 4, 512).transpose(2, 1, 0, 3)),
        wv4=C(W_in[:, 2 * D:].reshape(16, 128, 4, 512).transpose(2, 1, 0, 3)),
        wo4=C(W_out.reshape(16, 128, 4, 512).transpose(2, 1, 0, 3)),
        wg4=C(Wg.reshape(44, 128, 16, 128).transpose(0, 3, 2, 1)),
        wu4=C(Wu.reshape(44, 128, 16, 128).transpose(0, 3, 2, 1)),
        wd4=C(Wd.reshape(4, 512, 44, 128).transpose(0, 3, 2, 1)),
        b_in=C(np.asarray(L0_attn_b_in, dtype=np.float32)),
        b_out=C(np.asarray(L0_attn_b_out, dtype=np.float32)),
        gamma=C(np.asarray(L0_attn_gamma, dtype=np.float32)),
        beta=C(np.asarray(L0_attn_beta, dtype=np.float32)),
        s_g=C(np.asarray(L0_ffn_s_gate, dtype=np.float32)),
        s_u=C(np.asarray(L0_ffn_s_up, dtype=np.float32)),
        s_d=C(np.asarray(L0_ffn_s_down, dtype=np.float32)),
        ropc=ropc_np, rops=rops_np, ident=ident,
    )
    X = np.asarray(L0_attn_X, dtype=np.float32)
    nc = build_program()
    in_maps = []
    for c in range(8):
        m = dict(shared)
        m["x"] = np.ascontiguousarray(X[c])
        in_maps.append(m)
    res = run_bass_kernel_spmd(nc, in_maps, core_ids=list(range(8)))
    return np.stack([res.results[c]["out"] for c in range(8)], axis=0)


if __name__ == "__main__":
    nc = build_program()
    print("built ok, instructions:", nc.n_instructions())
